# version 19
# baseline (speedup 1.0000x reference)
import math
from contextlib import ExitStack
import numpy as np
import ml_dtypes
import concourse.bass as bass
import concourse.mybir as mybir
from concourse.bass_utils import run_bass_kernel_spmd

F32 = mybir.dt.float32
BF16 = mybir.dt.bfloat16
AF = mybir.ActivationFunctionType
ALU = mybir.AluOpType

D = 1024
DI = 2048
NS = 16
LN2C = math.log(2.0)
EPS = 1e-5
HALFPI = math.pi / 2.0

ENGS = ("sync", "scalar", "vector", "tensor", "gpsimd")
_NC_CACHE = {}
DMAE = ("sync", "scalar", "gpsimd")


_NO_CHAIN = frozenset((
    "wait_ge", "wait_op", "sem_inc", "sem_clear", "maybe_wait_ge",
    "maybe_drain_then_inc", "maybe_nop_then_inc", "dma_start",
    "dma_start_transpose", "nop", "drain", "engine_nop",
))


class _ChainEng:

    def __init__(self, e, sem, ctr):
        self._e = e
        self._sem = sem
        self._ctr = ctr

    def __getattr__(self, name):
        f = getattr(self._e, name)
        if name in _NO_CHAIN or not callable(f):
            return f

        def wrap(*a, **k):
            inst = f(*a, **k)
            if hasattr(inst, "then_inc"):
                if self._ctr[0] > 0:
                    inst._wait_ge(self._sem, self._ctr[0])
                self._ctr[0] += 1
                inst.then_inc(self._sem, 1)
            return inst
        return wrap


class Sched:

    NSLOT = {"sync": 10, "gpsimd": 6, "scalar": 4}

    def __init__(self, nc):
        self.nc = nc
        self.exec_sem = {e: nc.alloc_semaphore("x_" + e) for e in ENGS}
        self.exec_val = {e: 0 for e in ENGS}
        self.chain_sem = {e: nc.alloc_semaphore("c_" + e) for e in ENGS}
        self.chain_ctr = {e: [0] for e in ENGS}
        self.pool_sems = []
        self.pool_val = []
        self.pool_base = {}
        self.pool_rr = {}
        for e in DMAE:
            self.pool_base[e] = len(self.pool_sems)
            self.pool_rr[e] = 0
            for i in range(self.NSLOT[e]):
                self.pool_sems.append(nc.alloc_semaphore(f"dp_{e}{i}"))
                self.pool_val.append(0)
        self.bar_sem = nc.alloc_semaphore("bar")
        self.bar_val = 0
        self.stages = []
        self.last_w = {}
        self.readers = {}

    def _deps_for(self, reads, writes):
        deps = {}

        def need(tok):
            for key, v in tok:
                deps[key] = max(deps.get(key, 0), v)

        for b in reads:
            if b in self.last_w:
                need(self.last_w[b])
        for b in writes:
            if b in self.last_w:
                need(self.last_w[b])
            for tok in self.readers.get(b, ()):
                need(tok)
        return deps

    def _commit(self, reads, writes, tok):
        for b in reads:
            self.readers.setdefault(b, []).append(tok)
        for b in writes:
            self.last_w[b] = tok
            self.readers[b] = []

    def stage(self, eng, emit, dma=False):
        assert not dma
        deps = self._deps_for(emit.reads, emit.writes)
        self.exec_val[eng] += 1
        tok = [(("exec", eng), self.exec_val[eng])]
        self._commit(emit.reads, emit.writes, tok)
        self.stages.append((eng, emit, deps, False, None))

    def dma(self, eng, dmas, reads, writes):
        deps = self._deps_for(reads, writes)
        plan = []
        for _ in dmas:
            gid = self.pool_base[eng] + self.pool_rr[eng]
            self.pool_rr[eng] = (self.pool_rr[eng] + 1) % self.NSLOT[eng]
            prev = self.pool_val[gid]
            self.pool_val[gid] = prev + 16
            plan.append((gid, prev, prev + 16))
        tok = [(("dmap", gid), after) for (gid, prev, after) in plan]

        def emit(e, dmas=tuple(dmas), plan=tuple(plan), sched=self):
            for (dst, src), (gid, prev, after) in zip(dmas, plan):
                inst = e.dma_start(out=dst, in_=src)
                if prev > 0:
                    inst._wait_ge(sched.pool_sems[gid], prev)
                inst.then_inc(sched.pool_sems[gid], 16)
            return None
        emit.reads = tuple(reads)
        emit.writes = tuple(writes)
        self._commit(reads, writes, tok)
        self.stages.append((eng, emit, deps, True, plan))

    def run(self, blk):
        bodies = {e: [s for s in self.stages if s[0] == e] for e in ENGS}
        sched = self
        tensor_val_now = self.exec_val["tensor"]
        pool_now = list(self.pool_val)
        self.bar_val += 4

        def make_body(ename):
            def body(e):
                bound = {}
                ep = (_ChainEng(e, sched.chain_sem[ename],
                                sched.chain_ctr[ename])
                      if ename in ("vector", "scalar", "gpsimd") else e)
                for (eng, emit, deps, dma, plan) in bodies[ename]:
                    for key in sorted(deps, key=repr):
                        v = deps[key]
                        if key[0] == "exec" and key[1] == ename:
                            continue
                        if bound.get(key, 0) >= v:
                            continue
                        sem = (sched.exec_sem[key[1]] if key[0] == "exec"
                               else sched.pool_sems[key[1]])
                        e.wait_ge(sem, v)
                        bound[key] = v
                    last = emit(ep)
                    if not dma:
                        if ename in ("vector", "scalar", "gpsimd"):
                            ev = e.sem_inc(sched.exec_sem[ename], 1)
                            cc = sched.chain_ctr[ename][0]
                            if cc > 0:
                                ev._wait_ge(sched.chain_sem[ename], cc)
                        elif last is not None:
                            last.then_inc(sched.exec_sem[ename], 1)
                        else:
                            e.sem_inc(sched.exec_sem[ename], 1)
                if ename == "tensor":
                    e.wait_ge(sched.bar_sem, sched.bar_val)
                else:
                    if ename in DMAE:
                        base = sched.pool_base[ename]
                        for i in range(sched.NSLOT[ename]):
                            if pool_now[base + i] > 0:
                                e.wait_ge(sched.pool_sems[base + i],
                                          pool_now[base + i])
                    if tensor_val_now > 0:
                        e.wait_ge(sched.exec_sem["tensor"], tensor_val_now)
                    e.sem_inc(sched.bar_sem, 1)
                    e.wait_ge(sched.bar_sem, sched.bar_val)
            return body

        blk.sync(make_body("sync"))
        blk.scalar(make_body("scalar"))
        blk.vector(make_body("vector"))
        blk.tensor(make_body("tensor"))
        blk.gpsimd(make_body("gpsimd"))
        self.stages = []


def _em(fn, reads, writes, n_dma=None):
    fn.reads = tuple(reads)
    fn.writes = tuple(writes)
    if n_dma is not None:
        fn.n_dma = n_dma
    return fn



def build_nc(T=8192, C=512, CS=512, CF=512):
    nch = T // C
    nsc = T // CS
    ncf = T // CF
    nc = bass.Bass("TRN2", target_bir_lowering=False)
    S = Sched(nc)

    def din(name, shape, dt=F32):
        return nc.dram_tensor(name, shape, dt, kind="ExternalInput").ap()

    def scr(name, shape, dt=F32):
        return nc.dram_tensor(name, shape, dt, kind="Internal").ap()

    xin = din("xin", [128, 8, T])
    mask16 = din("mask16", [128, 8, T], BF16)
    l0_lnw = din("l0_lnw", [128, 8]); l0_lnb = din("l0_lnb", [128, 8])
    l0_win = din("l0_win", [D, 2 * DI]); l0_bin = din("l0_bin", [128, 32])
    l0_convw = din("l0_convw", [128, 16, 4]); l0_convb = din("l0_convb", [128, 16])
    l0_alog = din("l0_alog", [128, 16, NS]); l0_d = din("l0_d", [128, 16])
    l0_wx = din("l0_wx", [DI, 2 * NS]); l0_bx = din("l0_bx", [32, 1])
    l0_wout = din("l0_wout", [DI, D]); l0_bout = din("l0_bout", [128, 8])
    f0_w1t = din("f0_w1t", [D, 4 * D]); f0_b1 = din("f0_b1", [128, 32])
    f0_w2t = din("f0_w2t", [4 * D, D]); f0_b2 = din("f0_b2", [128, 8])
    l1_ln1g = din("l1_ln1g", [128, 8]); l1_ln1b = din("l1_ln1b", [128, 8])
    l1_lng = din("l1_lng", [128, 8]); l1_lnb = din("l1_lnb", [128, 8])
    l1_w1 = din("l1_w1", [D, D]); l1_w1b = din("l1_w1b", [128, 8])
    l1_v1 = din("l1_v1", [D, D]); l1_v1b = din("l1_v1b", [128, 8])
    l1_w2 = din("l1_w2", [D, D]); l1_w2b = din("l1_w2b", [128, 8])
    l1_convw = din("l1_convw", [128, 8, 3])
    l1_dtw = din("l1_dtw", [D, D]); l1_dtb = din("l1_dtb", [128, 8])
    l1_bpw = din("l1_bpw", [D, NS]); l1_bpb = din("l1_bpb", [16, 1])
    l1_cpw = din("l1_cpw", [D, NS]); l1_cpb = din("l1_cpb", [16, 1])
    l1_alog = din("l1_alog", [128, 8, NS]); l1_dp = din("l1_dp", [128, 8])
    l1_ln2g = din("l1_ln2g", [128, 8]); l1_ln2b = din("l1_ln2b", [128, 8])
    l1_ff1 = din("l1_ff1", [D, 3 * D]); l1_ff1b = din("l1_ff1b", [128, 24])
    l1_ff2 = din("l1_ff2", [3 * D, D]); l1_ff2b = din("l1_ff2b", [128, 8])
    bceye = din("bceye", [16, 16 * 128], BF16)
    l1f_th = din("l1f_th", [D, D]); l1f_ph = din("l1f_ph", [D, D])
    l1f_tn = din("l1f_tn", [D, D]); l1f_pn = din("l1f_pn", [D, D])
    out = nc.dram_tensor("out", [128, 8, T], F32, kind="ExternalOutput").ap()

    sc_xconv = scr("sc_xconv", [128, 16, T], BF16)
    sc_gate = scr("sc_gate", [128, 16, T], BF16)
    sc_b0 = scr("sc_b0", [32, T], BF16)
    sc_y0 = scr("sc_y0", [128, 16, T])
    sc_r0 = scr("sc_r0", [128, 8, T])
    sc_s1 = scr("sc_s1", [128, 8, T])
    sc_hq = scr("sc_hq", [128, 32, T], BF16)
    sc_w2q = scr("sc_w2q", [4 * D, D], BF16)
    sc_s2 = scr("sc_s2", [128, 8, T])
    sc_xh1 = scr("sc_xh1", [128, 8, T])
    sc_x1 = scr("sc_x1", [128, 8, T])
    sc_delta = scr("sc_delta", [128, 8, T])
    sc_bc1 = scr("sc_bc1", [16, 2 * T], BF16)
    sc_v = scr("sc_v", [128, 8, T], BF16)
    sc_y1 = scr("sc_y1", [128, 8, T])
    sc_s3 = scr("sc_s3", [128, 8, T])
    sc_s4 = scr("sc_s4", [128, 8, T])
    sc_row = scr("sc_row", [1, 2 * C])

    def persist(name, shape, dt=F32):
        return nc.alloc_sbuf_tensor(name, shape, dt).ap()

    gv = persist("gv", [128, 304])
    off = [0]

    def col(n):
        a = gv[:, off[0]:off[0] + n]
        off[0] += n
        return a

    c_l0lnw = col(8); c_l0lnb = col(8)
    c_bin = col(32); c_convb = col(16); c_d0 = col(16)
    c_bout = col(8); c_b1q = col(32); c_b2q = col(8)
    c_ln1g = col(8); c_ln1b = col(8); c_lng = col(8); c_lnb = col(8)
    c_w1b = col(8); c_v1b = col(8); c_w2b = col(8)
    c_dtb = col(8); c_dp = col(8); c_ln2g = col(8); c_ln2b = col(8)
    c_ff1b = col(24); c_ff2b = col(8)
    c_b1r = col(32); c_b2r = col(8)
    assert off[0] <= 304

    ga0 = persist("ga0", [128, 16 * NS])
    gst0 = persist("gst0", [128, 16 * NS])
    gA1 = persist("gA1", [128, 8 * NS])
    gst1 = persist("gst1", [128, 8 * NS])
    gcw0 = persist("gcw0", [128, 16 * 4])
    gcw1 = persist("gcw1", [128, 8 * 3])
    gbx = persist("gbx", [32, 4])
    gones = persist("gones", [128, 1])
    gfqt = persist("gfqt", [128, 2 * 32])
    gk = persist("gk", [128, 8])
    K_NEG1 = gk[:, 0:1]; K_NLN2 = gk[:, 1:2]; K_2LN2 = gk[:, 2:3]
    K_LN2 = gk[:, 3:4]; K_N2LN2 = gk[:, 4:5]; K_HALFPI = gk[:, 5:6]
    K_ONE = gk[:, 6:7]

    def emit_fq(src_key, src_ap, dst_ap, tmp_ap, signed, dst_key, tmp_key,
                ties_even=False):
        shp = list(src_ap.shape)
        W = 1
        for s in shp[1:]:
            W *= s

        def tview(k):
            t = tmp_ap[:, k * W:(k + 1) * W]
            if len(shp) == 3:
                t = t.rearrange("p (a c) -> p a c", c=shp[2])
            return t
        t_m, t_t = tview(0), tview(1)
        tk = tmp_key
        CQ = float(2 ** 21 + 1)

        def v_q(e):
            if signed:
                e.tensor_scalar_mul(out=t_m, in0=src_ap, scalar1=CQ)
                e.tensor_tensor(out=t_t, in0=t_m, in1=src_ap, op=ALU.subtract)
            else:
                e.tensor_scalar(out=dst_ap, in0=src_ap, scalar1=0.0,
                                scalar2=None, op0=ALU.max)
                e.tensor_scalar_mul(out=t_m, in0=dst_ap, scalar1=CQ)
                e.tensor_tensor(out=t_t, in0=t_m, in1=dst_ap, op=ALU.subtract)
            return e.tensor_tensor(out=dst_ap, in0=t_m, in1=t_t,
                                   op=ALU.subtract)
        S.stage("vector", _em(v_q, [src_key], [dst_key, tk]))

    def emit_ln(name, x_ap, ntile, bufs, second=False, xkey=None):
        sq = bufs["sq"]; sta = bufs["sta"]; stb = bufs["stb"]
        mu = bufs["mu"]; vp = bufs["vp"]; q = bufs["q"]; f = bufs["f"]
        rows = bufs["rows"]; rep = bufs["rep"]; tag = bufs["tag"]
        kx = xkey or (name + "_x")
        ksq, kst, krow, krep = (tag + s for s in
                                ("_sq", "_st", "_rows", "_rep"))
        Cw = x_ap.shape[2]

        def s_sq(e):
            return e.activation(out=sq.rearrange("p a c -> p (a c)"),
                                in_=x_ap.rearrange("p a c -> p (a c)"),
                                func=AF.Square)
        S.stage("scalar", _em(s_sq, [kx], [ksq]))

        def t_st(e):
            for dt in range(ntile):
                e.matmul(sta, gones, x_ap[:, dt, :],
                         start=(dt == 0), stop=(dt == ntile - 1))
            last = None
            for dt in range(ntile):
                last = e.matmul(stb, gones, sq[:, dt, :],
                                start=(dt == 0), stop=(dt == ntile - 1))
            return last
        S.stage("tensor", _em(t_st, [kx, ksq, "gones"], [kst, krow]))

        def v_r1(e):
            e.tensor_scalar_mul(out=mu, in0=sta, scalar1=1.0 / 1024.0)
            e.tensor_scalar(out=vp, in0=stb, scalar1=1.0 / 1024.0, scalar2=EPS,
                            op0=ALU.mult, op1=ALU.add)
            e.tensor_tensor(out=q, in0=mu, in1=mu, op=ALU.mult)
            return e.tensor_tensor(out=vp, in0=vp, in1=q, op=ALU.subtract)
        S.stage("vector", _em(v_r1, [kst], [krow]))

        def s_r1(e):
            return e.activation(out=q, in_=vp, func=AF.Sqrt)
        S.stage("scalar", _em(s_r1, [krow], [krow]))

        def v_r2(e):
            inv = rows[0:1, 0:Cw]
            t = rows[0:1, Cw:2 * Cw]
            e.reciprocal(out=inv, in_=q)
            e.tensor_tensor(out=t, in0=inv, in1=inv, op=ALU.mult)
            e.tensor_tensor(out=t, in0=t, in1=vp, op=ALU.mult)
            e.tensor_scalar(out=t, in0=t, scalar1=-0.5, scalar2=1.5,
                            op0=ALU.mult, op1=ALU.add)
            last = e.tensor_tensor(out=inv, in0=inv, in1=t, op=ALU.mult)
            if second:
                e.reciprocal(out=f, in_=vp)
                last = e.tensor_scalar(out=f, in0=f, scalar1=-EPS,
                                       scalar2=1.0 + EPS,
                                       op0=ALU.mult, op1=ALU.add)
            return last
        S.stage("vector", _em(v_r2, [krow], [krow]))

        if second:
            def s_r2(e):
                return e.activation(out=q, in_=f, func=AF.Sqrt)
            S.stage("scalar", _em(s_r2, [krow], [krow]))

            def v_r3(e):
                inv = rows[0:1, 0:Cw]
                t = rows[0:1, Cw:2 * Cw]
                q2 = vp
                e.reciprocal(out=q2, in_=q)
                e.tensor_tensor(out=t, in0=q2, in1=q2, op=ALU.mult)
                e.tensor_tensor(out=t, in0=t, in1=f, op=ALU.mult)
                e.tensor_scalar(out=t, in0=t, scalar1=-0.5, scalar2=1.5,
                                op0=ALU.mult, op1=ALU.add)
                e.tensor_tensor(out=q2, in0=q2, in1=t, op=ALU.mult)
                return e.tensor_tensor(out=inv, in0=inv, in1=q2, op=ALU.mult)
            S.stage("vector", _em(v_r3, [krow], [krow]))

        def v_r4(e):
            inv = rows[0:1, 0:Cw]
            b = rows[0:1, Cw:2 * Cw]
            e.tensor_tensor(out=b, in0=mu, in1=inv, op=ALU.mult)
            return e.tensor_scalar_mul(out=b, in0=b, scalar1=-1.0)
        S.stage("vector", _em(v_r4, [krow], [krow]))

        S.dma("sync", [(sc_row[:, 0:2 * Cw], rows[0:1, 0:2 * Cw])],
              [krow], ["sc_row"])
        S.dma("sync", [(rep, sc_row[0:1, 0:2 * Cw].partition_broadcast(128))],
              ["sc_row"], [krep])

    def emit_norm(name, x_ap, rep, xh_ap, ntile, tag, xh_key=None, xkey=None):
        kx, krep = xkey or (name + "_x"), tag + "_rep"
        kxh = xh_key or (name + "_xh")
        Cw = x_ap.shape[2]

        def v_n(e):
            s3 = rep[:, 0:Cw].rearrange("p (a c) -> p a c", c=Cw) \
                .broadcast_to([128, ntile, Cw])
            b3 = rep[:, Cw:2 * Cw].rearrange("p (a c) -> p a c", c=Cw) \
                .broadcast_to([128, ntile, Cw])
            e.tensor_tensor(out=xh_ap, in0=x_ap, in1=s3, op=ALU.mult)
            return e.tensor_tensor(out=xh_ap, in0=xh_ap, in1=b3, op=ALU.add)
        S.stage("vector", _em(v_n, [kx, krep], [kxh]))

    def dma_stage(eng, dmas, reads, writes):
        S.dma(eng, dmas, reads, writes)

    dma_stage("sync", [
        (c_l0lnw, l0_lnw), (c_l0lnb, l0_lnb), (c_bin, l0_bin),
        (c_convb, l0_convb), (c_d0, l0_d), (c_bout, l0_bout),
        (c_b1r, f0_b1), (c_b2r, f0_b2),
        (c_ln1g, l1_ln1g), (c_ln1b, l1_ln1b), (c_lng, l1_lng),
        (c_lnb, l1_lnb), (c_w1b, l1_w1b), (c_v1b, l1_v1b),
        (c_w2b, l1_w2b), (c_dtb, l1_dtb), (c_dp, l1_dp),
        (c_ln2g, l1_ln2g), (c_ln2b, l1_ln2b), (c_ff1b, l1_ff1b),
        (c_ff2b, l1_ff2b),
        (ga0, l0_alog.rearrange("p a n -> p (a n)")),
        (gA1, l1_alog.rearrange("p a n -> p (a n)")),
        (gcw0, l0_convw.rearrange("p a k -> p (a k)")),
        (gcw1, l1_convw.rearrange("p a k -> p (a k)")),
        (gbx[:, 0:1], l0_bx),
        (gbx[0:16, 1:2], l1_bpb), (gbx[0:16, 2:3], l1_cpb),
    ], [], ["gv", "ga0", "gA1", "gcw0", "gcw1", "gbx"])

    def p0_v(e):
        e.memset(gst0, 0.0)
        e.memset(gst1, 0.0)
        for i, v in enumerate([-1.0, -LN2C, 2.0 * LN2C, LN2C, -2.0 * LN2C,
                               HALFPI, 1.0]):
            e.memset(gk[:, i:i + 1], v)
        return e.memset(gones, 1.0)
    S.stage("vector", _em(p0_v, [], ["gst0", "gst1", "gones", "gk"]))

    def p0_s(e):
        e.activation(out=ga0, in_=ga0, func=AF.Exp)
        e.activation(out=ga0, in_=ga0, func=AF.Exp, scale=K_NEG1)
        return e.activation(out=gA1, in_=gA1, func=AF.Exp)
    S.stage("scalar", _em(p0_s, ["ga0", "gA1", "gk"], ["ga0", "gA1"]))

    def p0_v2(e):
        return e.tensor_scalar_mul(out=gA1, in0=gA1, scalar1=-1.0)
    S.stage("vector", _em(p0_v2, ["gA1"], ["gA1"]))

    emit_fq("gv", c_b1r, c_b1q, gfqt, True, "gv", "gfqt")
    emit_fq("gv", c_b2r, c_b2q, gfqt, True, "gv", "gfqt")

    with ExitStack() as _stk:
        h_win = _stk.enter_context(nc.sbuf_tensor("a_win", [128, 8, 2 * DI], BF16))
        h_wx = _stk.enter_context(nc.sbuf_tensor("a_wx", [128, 16, 32], F32))
        h_x = _stk.enter_context(nc.sbuf_tensor("a_x", [128, 8, C], F32))
        h_sq = _stk.enter_context(nc.sbuf_tensor("a_sq", [128, 8, C], F32))
        h_xn = _stk.enter_context(nc.sbuf_tensor("a_xn", [128, 8, C], BF16))
        h_rows = _stk.enter_context(nc.sbuf_tensor("a_rows", [1, 2 * C], F32))
        h_mu = _stk.enter_context(nc.sbuf_tensor("a_mu", [1, C], F32))
        h_vp = _stk.enter_context(nc.sbuf_tensor("a_vp", [1, C], F32))
        h_q = _stk.enter_context(nc.sbuf_tensor("a_q", [1, C], F32))
        h_rep = _stk.enter_context(nc.sbuf_tensor("a_rep", [128, 2 * C], F32))
        h_xm = _stk.enter_context(nc.sbuf_tensor("a_xm", [128, 16, C + 3], F32))
        h_cv = _stk.enter_context(nc.sbuf_tensor("a_cv", [128, 16, C], F32))
        h_g16 = _stk.enter_context(nc.sbuf_tensor("a_g16", [128, 16, C], BF16))
        h_bc = _stk.enter_context(nc.sbuf_tensor("a_bc", [32, C], BF16))
        h_sta = _stk.enter_context(nc.psum_tensor("a_sta", [1, C], F32))
        h_stb = _stk.enter_context(nc.psum_tensor("a_stb", [1, C], F32))
        h_ps0 = _stk.enter_context(nc.psum_tensor("a_ps0", [128, C], F32))
        h_ps1 = _stk.enter_context(nc.psum_tensor("a_ps1", [128, C], F32))
        h_bps = _stk.enter_context(nc.psum_tensor("a_bps", [32, C], F32))
        blk = _stk.enter_context(nc.Block())
        a_win = h_win.ap(); a_wx = h_wx.ap(); a_x = h_x.ap(); a_sq = h_sq.ap()
        a_xh = a_x; a_rb = a_sq; a_xn = h_xn.ap()
        a_rows = h_rows.ap(); a_mu = h_mu.ap(); a_vp = h_vp.ap()
        a_q = h_q.ap(); a_f = h_stb.ap(); a_rep = h_rep.ap(); a_xm = h_xm.ap()
        a_cv = h_cv.ap(); a_g16 = h_g16.ap()
        a_bc = h_bc.ap()
        a_pss = [h_ps0.ap(), h_ps1.ap()]
        a_bps = h_bps.ap()
        lnb = {"tag": "lnA", "sq": a_sq, "sta": h_sta.ap(), "stb": h_stb.ap(), "mu": a_mu,
               "vp": a_vp, "q": a_q, "f": a_f, "rows": a_rows, "rep": a_rep}

        dma_stage("gpsimd", [
            (a_win, l0_win.rearrange("(a b) m -> b a m", b=128)),
            (a_wx, l0_wx.rearrange("(a b) m -> b a m", b=128)),
        ], [], ["a_win", "a_wx"])

        def v_xm0(e):
            return e.memset(a_xm[:, :, 0:3], 0.0)
        S.stage("vector", _em(v_xm0, [], ["A_xm"]))

        for j in range(nch):
            sl = slice(j * C, (j + 1) * C)
            dma_stage("sync", [(a_x, xin[:, :, sl])], [], ["A_x"])
            emit_ln("A", a_x, 8, lnb, second=True)

            def v_rb(e):
                s3 = a_rep[:, 0:C].rearrange("p (a c) -> p a c", c=C) \
                    .broadcast_to([128, 8, C])
                b3 = a_rep[:, C:2 * C].rearrange("p (a c) -> p a c", c=C) \
                    .broadcast_to([128, 8, C])
                e.tensor_tensor(out=a_rb, in0=a_x, in1=s3, op=ALU.mult)
                e.tensor_tensor(out=a_rb, in0=a_rb, in1=b3, op=ALU.add)
                return e.tensor_tensor(out=a_rb, in0=a_rb, in1=a_x,
                                       op=ALU.add)
            S.stage("vector", _em(v_rb, ["A_x", "lnA_rep"], ["lnA_sq"]))
            dma_stage("scalar", [(sc_r0[:, :, sl], a_rb)], ["lnA_sq"], ["sc_r0"])
            emit_norm("A", a_x, a_rep, a_xh, 8, "lnA", xh_key="A_x")

            def s_xn(e):
                last = None
                for dt in range(8):
                    last = e.activation(out=a_xn[:, dt, :], in_=a_xh[:, dt, :],
                                        func=AF.Identity,
                                        scale=c_l0lnw[:, dt:dt + 1],
                                        bias=c_l0lnb[:, dt:dt + 1])
                return last
            S.stage("scalar", _em(s_xn, ["A_x", "gv"], ["A_xn"]))

            for m in range(32):
                ps = a_pss[m % 2]
                kps = "A_ps%d" % (m % 2)

                def t_mm(e, m=m, ps=ps):
                    last = None
                    for k in range(8):
                        last = e.matmul(ps, a_win[:, k, m * 128:(m + 1) * 128],
                                        a_xn[:, k, :], start=(k == 0),
                                        stop=(k == 7))
                    return last
                S.stage("tensor", _em(t_mm, ["A_xn", "a_win"], [kps]))

                def s_ev(e, m=m, ps=ps):
                    if m < 16:
                        return e.activation(out=a_xm[:, m, 3:], in_=ps,
                                            func=AF.Identity, scale=1.0,
                                            bias=c_bin[:, m:m + 1])
                    return e.activation(out=a_g16[:, m - 16, :], in_=ps,
                                        func=AF.Silu, scale=1.0,
                                        bias=c_bin[:, m:m + 1])
                wkey = "A_xm" if m < 16 else "A_g16"
                S.stage("scalar", _em(s_ev, [kps, "gv"], [wkey]))

            def v_cv(e):
                for it in range(16):
                    e.tensor_scalar(out=a_cv[:, it, :], in0=a_xm[:, it, 0:C],
                                    scalar1=gcw0[:, 4 * it:4 * it + 1],
                                    scalar2=None, op0=ALU.mult)
                    for k in range(1, 4):
                        e.scalar_tensor_tensor(
                            out=a_cv[:, it, :], in0=a_xm[:, it, k:k + C],
                            scalar=gcw0[:, 4 * it + k:4 * it + k + 1],
                            in1=a_cv[:, it, :], op0=ALU.mult, op1=ALU.add)
                return e.tensor_copy(out=a_xm[:, :, 0:3],
                                     in_=a_xm[:, :, C:C + 3])
            S.stage("vector", _em(v_cv, ["A_xm", "gcw0"], ["A_cv", "A_xm"]))

            def s_cv(e):
                last = None
                for it in range(16):
                    last = e.activation(out=a_cv[:, it, :], in_=a_cv[:, it, :],
                                        func=AF.Silu, scale=1.0,
                                        bias=c_convb[:, it:it + 1])
                return last
            S.stage("scalar", _em(s_cv, ["A_cv", "gv"], ["A_cv"]))

            def t_wx(e):
                last = None
                for k in range(16):
                    last = e.matmul(a_bps, a_wx[:, k, :], a_cv[:, k, :],
                                    start=(k == 0), stop=(k == 15))
                return last
            S.stage("tensor", _em(t_wx, ["A_cv", "a_wx"], ["A_bps"]))

            def s_bc(e):
                return e.activation(out=a_bc, in_=a_bps, func=AF.Identity,
                                    scale=1.0, bias=gbx[:, 0:1])
            S.stage("scalar", _em(s_bc, ["A_bps", "gbx"], ["A_bc"]))

            dma_stage("gpsimd", [(sc_xconv[:, :, sl], a_cv)],
                      ["A_cv"], ["sc_xconv"])
            dma_stage("scalar", [
                (sc_gate[:, :, sl], a_g16),
                (sc_b0[:, sl], a_bc),
            ], ["A_g16", "A_bc"], ["sc_gate", "sc_b0"])

        S.run(blk)

    with ExitStack() as _stk:
        h_bxc = _stk.enter_context(nc.sbuf_tensor("b_xc16", [128, 2, 16, CS], BF16))
        h_bb = _stk.enter_context(nc.sbuf_tensor("b_brep", [128, 2, 16, CS], BF16))
        h_bcr = _stk.enter_context(nc.sbuf_tensor("b_crep", [128, 2, 16, CS], BF16))
        h_bu = _stk.enter_context(nc.sbuf_tensor("b_u", [128, CS], BF16))
        NDVE = 8
        h_bh = _stk.enter_context(nc.sbuf_tensor("b_h", [128, 2, NDVE, CS], F32))
        h_bhv = _stk.enter_context(nc.sbuf_tensor("b_hv", [128, 16 - NDVE, CS], F32))
        h_bua = _stk.enter_context(nc.sbuf_tensor("b_ua", [128, 16, CS], BF16))
        h_bz = _stk.enter_context(nc.sbuf_tensor("b_z", [128, 16 - NDVE, CS], F32))
        h_bzp = _stk.enter_context(nc.sbuf_tensor("b_zp", [128, CS], F32))
        h_byp = _stk.enter_context(nc.sbuf_tensor("b_yp", [128, CS], F32))
        h_by0 = _stk.enter_context(nc.sbuf_tensor("b_y0", [128, CS], F32))
        h_by1 = _stk.enter_context(nc.sbuf_tensor("b_y1", [128, CS], F32))
        blk = _stk.enter_context(nc.Block())
        b_xc16 = h_bxc.ap(); b_brep = h_bb.ap(); b_crep = h_bcr.ap()
        b_u = h_bu.ap(); b_h = h_bh.ap(); b_hv = h_bhv.ap()
        b_ua = h_bua.ap(); b_z = h_bz.ap()
        b_zp = h_bzp.ap(); b_yp = h_byp.ap()
        b_ys = [h_by0.ap(), h_by1.ap()]

        for j in range(nsc):
            sl = slice(j * CS, (j + 1) * CS)
            jp = j % 2
            xc_j = b_xc16[:, jp, :, :]
            br_j = b_brep[:, jp, :, :]
            cr_j = b_crep[:, jp, :, :]
            kxc = "B_xc%d" % jp
            krep = "B_rep%d" % jp
            dma_stage("sync", [(xc_j, sc_xconv[:, :, sl])],
                      ["sc_xconv"], [kxc])
            dma_stage("sync",
                      [(br_j[:, n, :],
                        sc_b0[n:n + 1, sl].partition_broadcast(128))
                       for n in range(16)] +
                      [(cr_j[:, n, :],
                        sc_b0[16 + n:17 + n, sl].partition_broadcast(128))
                       for n in range(16)],
                      ["sc_b0"], [krep])

            for it in range(16):
                p = it % 2
                yb = b_ys[p]
                ky = "B_y%d" % p
                kh = "B_h%d" % p
                hcur = b_h[:, p, :, :]

                def v_scan(e, it=it, yb=yb, hcur=hcur, xc_j=xc_j,
                           br_j=br_j, cr_j=cr_j):
                    e.tensor_tensor(
                        out=b_ua,
                        in0=xc_j[:, it:it + 1, :].broadcast_to([128, 16, CS]),
                        in1=br_j, op=ALU.mult)
                    for n in range(16):
                        cidx = 16 * it + n
                        hdst = b_hv[:, n - NDVE, :] if n >= NDVE \
                            else hcur[:, n, :]
                        e.tensor_tensor_scan(
                            out=hdst,
                            data0=ga0[:, cidx:cidx + 1].broadcast_to([128, CS]),
                            data1=b_ua[:, n, :], initial=gst0[:, cidx:cidx + 1],
                            op0=ALU.mult, op1=ALU.add)
                        if n == NDVE - 1:
                            e.tensor_copy(
                                out=gst0[:, 16 * it:16 * it + NDVE]
                                .rearrange("p (a c) -> p a c", c=1),
                                in_=hcur[:, :, CS - 1:CS])
                    e.tensor_copy(
                        out=gst0[:, 16 * it + NDVE:16 * it + 16]
                        .rearrange("p (a c) -> p a c", c=1),
                        in_=b_hv[:, 0:16 - NDVE, CS - 1:CS])
                    zf = b_z.rearrange("p a c -> p (a c)")
                    e.tensor_tensor(out=b_z, in0=b_hv,
                                    in1=cr_j[:, NDVE:16, :], op=ALU.mult)
                    e.tensor_tensor(out=zf[:, 0:4 * CS], in0=zf[:, 0:4 * CS],
                                    in1=zf[:, 4 * CS:8 * CS], op=ALU.add)
                    e.tensor_tensor(out=zf[:, 0:2 * CS], in0=zf[:, 0:2 * CS],
                                    in1=zf[:, 2 * CS:4 * CS], op=ALU.add)
                    return e.tensor_tensor(out=yb, in0=zf[:, 0:CS],
                                           in1=zf[:, CS:2 * CS], op=ALU.add)
                S.stage("vector", _em(v_scan,
                                      [kxc, krep, "ga0", "gst0", ky, kh],
                                      [ky, kh, "gst0"]))

                def g_con(e, it=it, yb=yb, hcur=hcur, cr_j=cr_j):
                    e.tensor_tensor(out=hcur, in0=hcur,
                                    in1=cr_j[:, 0:NDVE, :], op=ALU.mult)
                    hf = hcur.rearrange("p a c -> p (a c)")
                    e.tensor_tensor(out=hf[:, 0:4 * CS], in0=hf[:, 0:4 * CS],
                                    in1=hf[:, 4 * CS:8 * CS], op=ALU.add)
                    e.tensor_tensor(out=hf[:, 0:2 * CS], in0=hf[:, 0:2 * CS],
                                    in1=hf[:, 2 * CS:4 * CS], op=ALU.add)
                    e.tensor_tensor(out=hf[:, 0:CS], in0=hf[:, 0:CS],
                                    in1=hf[:, CS:2 * CS], op=ALU.add)
                    return e.tensor_tensor(out=yb, in0=yb, in1=hf[:, 0:CS],
                                           op=ALU.add)
                S.stage("gpsimd", _em(g_con, [kh, ky, krep], [ky, kh]))
                dma_stage("scalar", [(sc_y0[:, it, sl], yb)], [ky], ["sc_y0"])

        S.run(blk)

    with ExitStack() as _stk:
        C2 = C // 2
        h_cwo = _stk.enter_context(nc.sbuf_tensor("c_wo", [128, 16, D], BF16))
        h_cy = _stk.enter_context(nc.sbuf_tensor("c_y", [128, 2, 16, C2], F32))
        h_cxc = _stk.enter_context(nc.sbuf_tensor("c_xc", [128, 2, 16, C2], BF16))
        h_cg = _stk.enter_context(nc.sbuf_tensor("c_g16", [128, 2, 16, C2], BF16))
        h_cyg = _stk.enter_context(nc.sbuf_tensor("c_yg", [128, 16, C2], BF16))
        h_crb = _stk.enter_context(nc.sbuf_tensor("c_rb", [128, 2, 8, C2], F32))
        h_cs1 = _stk.enter_context(nc.sbuf_tensor("c_s1", [128, 2, 8, C2], F32))
        h_cps0 = _stk.enter_context(nc.psum_tensor("c_ps0", [128, C2], F32))
        h_cps1 = _stk.enter_context(nc.psum_tensor("c_ps1", [128, C2], F32))
        blk = _stk.enter_context(nc.Block())
        c_wo = h_cwo.ap(); c_y = h_cy.ap(); c_xc = h_cxc.ap()
        c_g16 = h_cg.ap(); c_yg = h_cyg.ap(); c_rb = h_crb.ap()
        c_s1 = h_cs1.ap(); c_pss = [h_cps0.ap(), h_cps1.ap()]

        dma_stage("gpsimd",
                  [(c_wo, l0_wout.rearrange("(a b) m -> b a m", b=128))],
                  [], ["c_wo"])

        for j in range(2 * nch):
            sl = slice(j * C2, (j + 1) * C2)
            jp = j % 2
            cy_j = c_y[:, jp, :, :]
            cxc_j = c_xc[:, jp, :, :]
            cg_j = c_g16[:, jp, :, :]
            crb_j = c_rb[:, jp, :, :]
            cs1_j = c_s1[:, jp, :, :]
            kld = "C_ld%d" % jp
            ks1 = "C_s1%d" % jp
            dma_stage("sync", [
                (cy_j, sc_y0[:, :, sl]),
                (cxc_j, sc_xconv[:, :, sl]),
                (cg_j, sc_gate[:, :, sl]),
                (crb_j, sc_r0[:, :, sl]),
            ], ["sc_y0", "sc_xconv", "sc_gate", "sc_r0"], [kld])

            def v_g(e, cy_j=cy_j, cxc_j=cxc_j, cg_j=cg_j):
                for it in range(16):
                    e.scalar_tensor_tensor(out=cy_j[:, it, :],
                                           in0=cxc_j[:, it, :],
                                           scalar=c_d0[:, it:it + 1],
                                           in1=cy_j[:, it, :],
                                           op0=ALU.mult, op1=ALU.add)
                return e.tensor_tensor(out=c_yg, in0=cy_j, in1=cg_j,
                                       op=ALU.mult)
            S.stage("vector", _em(v_g, [kld, "gv"], ["C_yg", kld]))

            for m in range(8):
                ps = c_pss[m % 2]
                kps = "C_ps%d" % (m % 2)

                def t_mm(e, m=m, ps=ps):
                    last = None
                    for k in range(16):
                        last = e.matmul(ps, c_wo[:, k, m * 128:(m + 1) * 128],
                                        c_yg[:, k, :], start=(k == 0),
                                        stop=(k == 15))
                    return last
                S.stage("tensor", _em(t_mm, ["C_yg", "c_wo"], [kps]))

                def v_ev(e, m=m, ps=ps, crb_j=crb_j, cs1_j=cs1_j):
                    return e.scalar_tensor_tensor(out=cs1_j[:, m, :], in0=ps,
                                                  scalar=c_bout[:, m:m + 1],
                                                  in1=crb_j[:, m, :],
                                                  op0=ALU.add, op1=ALU.add)
                S.stage("vector", _em(v_ev, [kps, kld, "gv"], [ks1]))

            dma_stage("scalar", [(sc_s1[:, :, sl], cs1_j)], [ks1], ["sc_s1"])

        S.run(blk)

    with ExitStack() as _stk:
        h_fw1 = _stk.enter_context(nc.sbuf_tensor("f_w1", [128, 8, 4 * D], BF16))
        h_fwt = _stk.enter_context(nc.sbuf_tensor("f_wt", [128, 8, 256], F32))
        h_ffqw = _stk.enter_context(nc.sbuf_tensor("f_fqw", [128, 2 * 2048], F32))
        h_fx = _stk.enter_context(nc.sbuf_tensor("f_x", [128, 2, 8, CF], F32))
        h_fsq = _stk.enter_context(nc.sbuf_tensor("f_sq", [128, 8, CF], F32))
        h_fxq = _stk.enter_context(nc.sbuf_tensor("f_xq", [128, 8, CF], BF16))
        h_frows = _stk.enter_context(nc.sbuf_tensor("f_rows", [1, 2 * CF], F32))
        h_fmu = _stk.enter_context(nc.sbuf_tensor("f_mu", [1, CF], F32))
        h_fvp = _stk.enter_context(nc.sbuf_tensor("f_vp", [1, CF], F32))
        h_fq = _stk.enter_context(nc.sbuf_tensor("f_q", [1, CF], F32))
        h_frep = _stk.enter_context(nc.sbuf_tensor("f_rep", [128, 2 * CF], F32))
        h_fh16 = _stk.enter_context(nc.sbuf_tensor("f_h16", [128, 32, CF], BF16))
        h_fsta = _stk.enter_context(nc.psum_tensor("f_sta", [1, CF], F32))
        h_fstb = _stk.enter_context(nc.psum_tensor("f_stb", [1, CF], F32))
        h_fps0 = _stk.enter_context(nc.psum_tensor("f_ps0", [128, CF], F32))
        h_fps1 = _stk.enter_context(nc.psum_tensor("f_ps1", [128, CF], F32))
        blk = _stk.enter_context(nc.Block())
        f_w1 = h_fw1.ap(); f_wt = h_fwt.ap(); f_fqw = h_ffqw.ap()
        f_x = h_fx.ap(); f_sq = h_fsq.ap()
        f_xq = h_fxq.ap(); f_rows = h_frows.ap()
        f_mu = h_fmu.ap(); f_vp = h_fvp.ap(); f_q = h_fq.ap(); f_f = h_fstb.ap()
        f_rep = h_frep.ap(); f_h16 = h_fh16.ap()
        f_pss = [h_fps0.ap(), h_fps1.ap()]
        lnbf = {"tag": "lnF", "sq": f_sq, "sta": h_fsta.ap(), "stb": h_fstb.ap(),
                "mu": f_mu, "vp": f_vp, "q": f_q, "f": f_f,
                "rows": f_rows, "rep": f_rep}

        for si in range(16):
            csl = slice(si * 256, (si + 1) * 256)
            dma_stage("sync",
                      [(f_wt, f0_w1t.rearrange("(a b) m -> b a m", b=128)[:, :, csl])],
                      [], ["F_wt"])
            emit_fq("F_wt", f_wt, f_w1[:, :, csl], f_fqw, True, "f_w1",
                    "F_fqw")

        for j in range(ncf):
            sl = slice(j * CF, (j + 1) * CF)
            jp = j % 2
            x_j = f_x[:, jp, :, :]
            kx = "F_x%d" % jp
            dma_stage("sync", [(x_j, sc_s1[:, :, sl])], ["sc_s1"], [kx])
            emit_ln("F", x_j, 8, lnbf, second=False, xkey=kx)
            emit_norm("F", x_j, f_rep, x_j, 8, "lnF", xh_key=kx, xkey=kx)
            emit_fq(kx, x_j[:, 0:4, :], f_xq[:, 0:4, :], f_fqw, True,
                    "F_xq", "F_fqw")
            emit_fq(kx, x_j[:, 4:8, :], f_xq[:, 4:8, :], f_fqw, True,
                    "F_xq", "F_fqw")

            for m in range(32):
                ps = f_pss[m % 2]
                kps = "F_ps%d" % (m % 2)

                def t_mm(e, m=m, ps=ps):
                    last = None
                    for k in range(8):
                        last = e.matmul(ps, f_w1[:, k, m * 128:(m + 1) * 128],
                                        f_xq[:, k, :], start=(k == 0),
                                        stop=(k == 7))
                    return last
                S.stage("tensor", _em(t_mm, ["F_xq", "f_w1"], [kps]))

                def s_ev(e, m=m, ps=ps):
                    return e.activation(out=f_h16[:, m, :], in_=ps,
                                        func=AF.Identity, scale=1.0,
                                        bias=c_b1q[:, m:m + 1])
                S.stage("scalar", _em(s_ev, [kps, "gv"], ["F_h16"]))

            for g in range(8):
                gsl = slice(g * 4, (g + 1) * 4)
                emit_fq("F_h16", f_h16[:, gsl, :], f_h16[:, gsl, :],
                        f_fqw, False, "F_h16", "F_fqw", ties_even=True)

            dma_stage("scalar", [(sc_hq[:, :, sl], f_h16)], ["F_h16"], ["sc_hq"])

        S.run(blk)

    with ExitStack() as _stk:
        h_gw2 = _stk.enter_context(nc.sbuf_tensor("g_w2", [128, 32, D], BF16))
        h_gwt = _stk.enter_context(nc.sbuf_tensor("g_wt", [128, 32, 64], F32))
        h_ghq = _stk.enter_context(nc.sbuf_tensor("g_hq", [128, 2, 32, C], BF16))
        h_gs1 = _stk.enter_context(nc.sbuf_tensor("g_s1", [128, 2, 8, C], F32))
        h_gmk = _stk.enter_context(nc.sbuf_tensor("g_mk", [128, 2, 8, C], BF16))
        h_gy2 = _stk.enter_context(nc.sbuf_tensor("g_y2", [128, 8, C], BF16))
        h_gps0 = _stk.enter_context(nc.psum_tensor("g_ps0", [128, C], F32))
        h_gps1 = _stk.enter_context(nc.psum_tensor("g_ps1", [128, C], F32))
        blk = _stk.enter_context(nc.Block())
        g_w2 = h_gw2.ap(); g_wt = h_gwt.ap()
        g_fqw = h_gs1.ap().rearrange("p a b c -> p (a b c)")[:, 0:4096]
        g_hq = h_ghq.ap(); g_s1 = h_gs1.ap(); g_mk = h_gmk.ap()
        g_y2 = h_gy2.ap()
        g_pss = [h_gps0.ap(), h_gps1.ap()]

        for si in range(16):
            csl = slice(si * 64, (si + 1) * 64)
            dma_stage("sync",
                      [(g_wt, f0_w2t.rearrange("(a b) m -> b a m", b=128)[:, :, csl])],
                      [], ["G_wt"])
            emit_fq("G_wt", g_wt, g_w2[:, :, csl], g_fqw, True, "g_w2",
                    "G_fqw")

        for j in range(nch):
            sl = slice(j * C, (j + 1) * C)
            jp = j % 2
            hq_j = g_hq[:, jp, :, :]
            s1_j = g_s1[:, jp, :, :]
            mk_j = g_mk[:, jp, :, :]
            khq = "G_hq%d" % jp
            ks1 = "G_s1%d" % jp
            dma_stage("sync", [
                (hq_j, sc_hq[:, :, sl]),
                (s1_j, sc_s1[:, :, sl]),
                (mk_j, mask16[:, :, sl]),
            ], ["sc_hq", "sc_s1"] + (["G_fqw", "g_w2"] if j == 0 else []),
                [khq, ks1])

            for m in range(8):
                ps = g_pss[m % 2]
                kps = "G_ps%d" % (m % 2)

                def t_mm(e, m=m, ps=ps, hq_j=hq_j):
                    last = None
                    for k in range(32):
                        last = e.matmul(ps, g_w2[:, k, m * 128:(m + 1) * 128],
                                        hq_j[:, k, :], start=(k == 0),
                                        stop=(k == 31))
                    return last
                S.stage("tensor", _em(t_mm, [khq, "g_w2"], [kps]))

                def s_ev(e, m=m, ps=ps):
                    return e.activation(out=g_y2[:, m, :], in_=ps,
                                        func=AF.Identity, scale=1.0,
                                        bias=c_b2q[:, m:m + 1])
                S.stage("scalar", _em(s_ev, [kps, "gv"], ["G_y2"]))

            def v_fin(e, s1_j=s1_j, mk_j=mk_j):
                e.tensor_tensor(out=g_y2, in0=g_y2, in1=mk_j, op=ALU.mult)
                return e.tensor_tensor(out=s1_j, in0=s1_j, in1=g_y2,
                                       op=ALU.add)
            S.stage("vector", _em(v_fin, ["G_y2", khq, ks1],
                                  ["G_y2", ks1]))

            dma_stage("scalar", [(sc_s2[:, :, sl], s1_j)], [ks1], ["sc_s2"])

        S.run(blk)

    with ExitStack() as _stk:
        h_lw1 = _stk.enter_context(nc.sbuf_tensor("l_w1", [128, 8, D], BF16))
        h_lv1 = _stk.enter_context(nc.sbuf_tensor("l_v1", [128, 8, D], BF16))
        h_lx = _stk.enter_context(nc.sbuf_tensor("l_x", [128, 2, 8, C], F32))
        h_lsq = _stk.enter_context(nc.sbuf_tensor("l_sq", [128, 8, C], F32))
        h_lxh = _stk.enter_context(nc.sbuf_tensor("l_xh", [128, 8, C], F32))
        h_lxn = _stk.enter_context(nc.sbuf_tensor("l_xn", [128, 8, C], F32))
        h_lxm = _stk.enter_context(nc.sbuf_tensor("l_xm", [128, 8, C], F32))
        h_lxm16 = _stk.enter_context(nc.sbuf_tensor("l_xm16", [128, 8, C], BF16))
        h_lrows = _stk.enter_context(nc.sbuf_tensor("l_rows", [1, 2 * C], F32))
        h_lmu = _stk.enter_context(nc.sbuf_tensor("l_mu", [1, C], F32))
        h_lvp = _stk.enter_context(nc.sbuf_tensor("l_vp", [1, C], F32))
        h_lq = _stk.enter_context(nc.sbuf_tensor("l_q", [1, C], F32))
        h_lrep = _stk.enter_context(nc.sbuf_tensor("l_rep", [128, 2 * C], F32))
        h_lx1 = _stk.enter_context(nc.sbuf_tensor("l_x1", [128, 8, C], F32))
        h_lv16 = _stk.enter_context(nc.sbuf_tensor("l_v16", [128, 8, C], BF16))
        h_lsta = _stk.enter_context(nc.psum_tensor("l_sta", [1, C], F32))
        h_lstb = _stk.enter_context(nc.psum_tensor("l_stb", [1, C], F32))
        h_lps0 = _stk.enter_context(nc.psum_tensor("l_ps0", [128, C], F32))
        h_lps1 = _stk.enter_context(nc.psum_tensor("l_ps1", [128, C], F32))
        blk = _stk.enter_context(nc.Block())
        l_w1 = h_lw1.ap(); l_v1 = h_lv1.ap()
        l_x = h_lx.ap(); l_sq = h_lsq.ap(); l_xh = h_lxh.ap()
        l_xn = h_lxn.ap(); l_xm = h_lxm.ap(); l_xm16 = h_lxm16.ap()
        l_rows = h_lrows.ap(); l_mu = h_lmu.ap(); l_vp = h_lvp.ap()
        l_q = h_lq.ap(); l_f = h_lstb.ap(); l_rep = h_lrep.ap()
        l_x1 = h_lx1.ap(); l_v16 = h_lv16.ap()
        l_pss = [h_lps0.ap(), h_lps1.ap()]
        lnb1 = {"tag": "lnL", "sq": l_sq, "sta": h_lsta.ap(), "stb": h_lstb.ap(),
                "mu": l_mu, "vp": l_vp, "q": l_q, "f": l_f,
                "rows": l_rows, "rep": l_rep}

        dma_stage("gpsimd", [
            (l_w1, l1_w1.rearrange("(a b) m -> b a m", b=128)),
            (l_v1, l1_v1.rearrange("(a b) m -> b a m", b=128)),
        ], [], ["l_w1", "l_v1"])

        for j in range(nch):
            sl = slice(j * C, (j + 1) * C)
            jp = j % 2
            lx_j = l_x[:, jp, :, :]
            klx = "L_x%d" % jp
            dma_stage("sync", [(lx_j, sc_s2[:, :, sl])], ["sc_s2"], [klx])
            emit_ln("L", lx_j, 8, lnb1, second=True, xkey=klx)
            emit_norm("L", lx_j, l_rep, l_xh, 8, "lnL", xkey=klx)
            dma_stage("scalar", [(sc_xh1[:, :, sl], l_xh)], ["L_xh"], ["sc_xh1"])

            def s_xn(e):
                last = None
                for dt in range(8):
                    last = e.activation(out=l_xn[:, dt, :], in_=l_xh[:, dt, :],
                                        func=AF.Identity,
                                        scale=c_ln1g[:, dt:dt + 1],
                                        bias=c_ln1b[:, dt:dt + 1])
                return last
            S.stage("scalar", _em(s_xn, ["L_xh", "gv"], ["M_x"]))

            emit_ln("M", l_xn, 8, lnb1, second=False)
            emit_norm("M", l_xn, l_rep, l_xm, 8, "lnL")

            def s_xm(e):
                last = None
                for dt in range(8):
                    last = e.activation(out=l_xm16[:, dt, :],
                                        in_=l_xm[:, dt, :],
                                        func=AF.Identity,
                                        scale=c_lng[:, dt:dt + 1],
                                        bias=c_lnb[:, dt:dt + 1])
                return last
            S.stage("scalar", _em(s_xm, ["M_xh", "gv"], ["M_xm"]))

            for wname, wap, bias_col, kind in (
                ("l_w1", l_w1, c_w1b, "x1"),
                ("l_v1", l_v1, c_v1b, "v"),
            ):
                for m in range(8):
                    ps = l_pss[m % 2]
                    kps = "L_ps%d" % (m % 2)

                    def t_mm(e, m=m, ps=ps, wap=wap):
                        last = None
                        for k in range(8):
                            last = e.matmul(ps, wap[:, k, m * 128:(m + 1) * 128],
                                            l_xm16[:, k, :], start=(k == 0),
                                            stop=(k == 7))
                        return last
                    S.stage("tensor", _em(t_mm, ["M_xm", wname], [kps]))

                    def s_ev(e, m=m, ps=ps, bias_col=bias_col, kind=kind):
                        if kind == "x1":
                            return e.activation(out=l_x1[:, m, :], in_=ps,
                                                func=AF.Identity, scale=1.0,
                                                bias=bias_col[:, m:m + 1])
                        return e.activation(out=l_v16[:, m, :], in_=ps,
                                            func=AF.Silu, scale=1.0,
                                            bias=bias_col[:, m:m + 1])
                    wk = {"x1": "L_x1", "v": "L_v16"}[kind]
                    S.stage("scalar", _em(s_ev, [kps, "gv"], [wk]))

            dma_stage("scalar", [(sc_x1[:, :, sl], l_x1)], ["L_x1"], ["sc_x1"])
            dma_stage("scalar", [(sc_v[:, :, sl], l_v16)], ["L_v16"], ["sc_v"])


        S.run(blk)

    with ExitStack() as _stk:
        h_mx1 = _stk.enter_context(nc.sbuf_tensor("m_x1h", [128, 8, CS + 2], F32))
        h_mdt = _stk.enter_context(nc.sbuf_tensor("m_dt", [128, 8, D], BF16))
        h_mbp = _stk.enter_context(nc.sbuf_tensor("m_bp", [128, 8, NS], BF16))
        h_mcp = _stk.enter_context(nc.sbuf_tensor("m_cp", [128, 8, NS], BF16))
        h_mxc16 = _stk.enter_context(nc.sbuf_tensor("m_xc16", [128, 8, CS], BF16))
        h_mbc = _stk.enter_context(nc.sbuf_tensor("m_bc16", [16, 2 * CS], BF16))
        h_meye = _stk.enter_context(nc.sbuf_tensor("m_eye", [16, 16 * 128], BF16))
        h_mpbc0 = _stk.enter_context(nc.psum_tensor("m_pbc0", [128, CS], F32))
        h_mpbc1 = _stk.enter_context(nc.psum_tensor("m_pbc1", [128, CS], F32))
        h_mps0 = _stk.enter_context(nc.psum_tensor("m_ps0", [128, CS], F32))
        h_mps1 = _stk.enter_context(nc.psum_tensor("m_ps1", [128, CS], F32))
        h_mpsb = _stk.enter_context(nc.psum_tensor("m_psb", [16, CS], F32))
        h_mpsc = _stk.enter_context(nc.psum_tensor("m_psc", [16, CS], F32))
        h_mdl = _stk.enter_context(nc.sbuf_tensor("m_dl", [128, 8, CS], F32))
        h_mcv = _stk.enter_context(nc.sbuf_tensor("m_cv", [128, 8, CS], F32))
        h_mw16 = _stk.enter_context(nc.sbuf_tensor("m_w16", [128, 8, CS], BF16))
        h_mb = _stk.enter_context(nc.sbuf_tensor("m_brep", [128, 16, CS], BF16))
        h_mc = _stk.enter_context(nc.sbuf_tensor("m_crep", [128, 16, CS], BF16))
        h_mda0 = _stk.enter_context(nc.sbuf_tensor("m_dA0", [128, CS], F32))
        h_mda1 = _stk.enter_context(nc.sbuf_tensor("m_dA1", [128, CS], F32))
        NPOOL1 = 10
        h_mh = _stk.enter_context(nc.sbuf_tensor("m_h", [128, 2, NPOOL1, CS], F32))
        h_mhv = _stk.enter_context(nc.sbuf_tensor("m_hv", [128, 16 - NPOOL1, CS], F32))
        h_mua = _stk.enter_context(nc.sbuf_tensor("m_ua", [128, 16, CS], BF16))
        h_mzp = _stk.enter_context(nc.sbuf_tensor("m_zp", [128, CS], F32))
        h_myp = _stk.enter_context(nc.sbuf_tensor("m_yp", [128, CS], F32))
        h_my0 = _stk.enter_context(nc.sbuf_tensor("m_y0", [128, CS], F32))
        h_my1 = _stk.enter_context(nc.sbuf_tensor("m_y1", [128, CS], F32))
        blk = _stk.enter_context(nc.Block())
        m_x1h = h_mx1.ap(); m_dl = h_mdl.ap(); m_cv = h_mcv.ap()
        m_dt = h_mdt.ap(); m_bp = h_mbp.ap(); m_cp = h_mcp.ap()
        m_xc16 = h_mxc16.ap(); m_sp = h_mh.ap()[:, 0, 0:8, :]; m_bc16 = h_mbc.ap()
        m_pss = [h_mps0.ap(), h_mps1.ap()]
        m_psb = h_mpsb.ap(); m_psc = h_mpsc.ap()
        m_w16 = h_mw16.ap(); m_brep = h_mb.ap(); m_crep = h_mc.ap()
        m_eye = h_meye.ap()
        m_pbcs = [h_mpbc0.ap(), h_mpbc1.ap()]
        m_dAs = [h_mda0.ap(), h_mda1.ap()]
        m_h = h_mh.ap(); m_hv = h_mhv.ap()
        m_ua = h_mua.ap()
        m_zp = h_mzp.ap(); m_yp = h_myp.ap()
        m_ys = [h_my0.ap(), h_my1.ap()]

        dma_stage("sync", [(m_eye, bceye)], [], ["m_eye"])
        dma_stage("gpsimd", [
            (m_dt, l1_dtw.rearrange("(a b) m -> b a m", b=128)),
            (m_bp, l1_bpw.rearrange("(a b) m -> b a m", b=128)),
            (m_cp, l1_cpw.rearrange("(a b) m -> b a m", b=128)),
        ], [], ["m_dt", "m_bp", "m_cp"])

        for j in range(nsc):
            sl = slice(j * CS, (j + 1) * CS)

            lo = max(0, j * CS - 1)
            hi = min(T, j * CS + CS + 1)
            d0 = 1 if j == 0 else 0
            d1 = CS + 1 if j == nsc - 1 else CS + 2
            if j == 0 or j == nsc - 1:
                def v_zh(e, j=j):
                    last = None
                    if j == 0:
                        last = e.memset(m_x1h[:, :, 0:1], 0.0)
                    if j == nsc - 1:
                        last = e.memset(m_x1h[:, :, CS + 1:CS + 2], 0.0)
                    return last
                S.stage("vector", _em(v_zh, [], ["M_x1h"]))
            dma_stage("sync", [(m_x1h[:, :, d0:d1], sc_x1[:, :, lo:hi])],
                      ["sc_x1"], ["M_x1h"])
            def v_cv(e):
                for dt in range(8):
                    e.tensor_scalar(out=m_cv[:, dt, :], in0=m_x1h[:, dt, 0:CS],
                                    scalar1=gcw1[:, 3 * dt:3 * dt + 1],
                                    scalar2=None, op0=ALU.mult)
                    for k in range(1, 3):
                        e.scalar_tensor_tensor(
                            out=m_cv[:, dt, :], in0=m_x1h[:, dt, k:k + CS],
                            scalar=gcw1[:, 3 * dt + k:3 * dt + k + 1],
                            in1=m_cv[:, dt, :], op0=ALU.mult, op1=ALU.add)
                return None
            S.stage("vector", _em(v_cv, ["M_x1h", "gcw1"], ["M_cv"]))

            def s_cv(e):
                return e.activation(out=m_cv.rearrange("p a c -> p (a c)"),
                                    in_=m_cv.rearrange("p a c -> p (a c)"),
                                    func=AF.Silu)
            S.stage("scalar", _em(s_cv, ["M_cv"], ["M_cv"]))

            def v_xc16(e):
                return e.tensor_copy(out=m_xc16.rearrange("p a c -> p (a c)"),
                                     in_=m_cv.rearrange("p a c -> p (a c)"))
            S.stage("vector", _em(v_xc16, ["M_cv"], ["M_xc16"]))

            for m in range(8):
                ps = m_pss[m % 2]
                kps = "M_ps%d" % (m % 2)

                def t_mm(e, m=m, ps=ps):
                    last = None
                    for k in range(8):
                        last = e.matmul(ps, m_dt[:, k, m * 128:(m + 1) * 128],
                                        m_xc16[:, k, :], start=(k == 0),
                                        stop=(k == 7))
                    return last
                S.stage("tensor", _em(t_mm, ["M_xc16", "m_dt"], [kps]))

                def s_ev(e, m=m, ps=ps):
                    return e.activation(out=m_dl[:, m, :], in_=ps,
                                        func=AF.Identity, scale=1.0,
                                        bias=c_dtb[:, m:m + 1])
                S.stage("scalar", _em(s_ev, [kps, "gv"], ["M_dl"]))

            def s_sp(e):
                fl = m_dl.rearrange("p a c -> p (a c)")
                t1 = m_sp.rearrange("p a c -> p (a c)")
                e.activation(out=t1, in_=fl, func=AF.Abs)
                e.activation(out=t1, in_=t1, func=AF.Exp, scale=K_NEG1)
                return e.activation(out=t1, in_=t1, func=AF.Ln, bias=K_ONE)
            S.stage("scalar", _em(s_sp, ["M_dl", "M_h0", "M_h1"],
                                  ["M_sp", "M_h0", "M_h1"]))

            def v_sp(e):
                fl = m_dl.rearrange("p a c -> p (a c)")
                t1 = m_sp.rearrange("p a c -> p (a c)")
                e.tensor_scalar_max(out=fl, in0=fl, scalar1=0.0)
                return e.tensor_tensor(out=fl, in0=fl, in1=t1, op=ALU.add)
            S.stage("vector", _em(v_sp, ["M_dl", "M_sp"], ["M_dl"]))

            def t_bp(e):
                for k in range(8):
                    e.matmul(m_psb, m_bp[:, k, :], m_xc16[:, k, :],
                             start=(k == 0), stop=(k == 7))
                last = None
                for k in range(8):
                    last = e.matmul(m_psc, m_cp[:, k, :], m_xc16[:, k, :],
                                    start=(k == 0), stop=(k == 7))
                return last
            S.stage("tensor", _em(t_bp, ["M_xc16", "m_bp", "m_cp"], ["M_psbc"]))

            def s_bc(e):
                e.activation(out=m_bc16[:, 0:CS], in_=m_psb, func=AF.Identity,
                             scale=1.0, bias=gbx[0:16, 1:2])
                return e.activation(out=m_bc16[:, CS:2 * CS], in_=m_psc,
                                    func=AF.Identity, scale=1.0,
                                    bias=gbx[0:16, 2:3])
            S.stage("scalar", _em(s_bc, ["M_psbc", "gbx"], ["M_bc"]))

            dma_stage("scalar", [
                (sc_bc1[:, j * CS:(j + 1) * CS], m_bc16[:, 0:CS]),
                (sc_bc1[:, T + j * CS:T + (j + 1) * CS], m_bc16[:, CS:2 * CS]),
            ], ["M_bc"], ["sc_bc1"])
            for half, rep in ((0, m_brep), (1, m_crep)):
                for n in range(16):
                    pb = m_pbcs[n % 2]
                    kpb = "M_pbc%d" % (n % 2)

                    def t_bc(e, n=n, half=half, pb=pb):
                        return e.matmul(
                            pb, m_eye[:, n * 128:(n + 1) * 128],
                            m_bc16[:, half * CS:half * CS + CS],
                            start=True, stop=True)
                    S.stage("tensor", _em(t_bc, ["M_bc", "m_eye", kpb],
                                          [kpb]))

                    def s_bc2(e, n=n, pb=pb, rep=rep):
                        return e.activation(out=rep[:, n, :], in_=pb,
                                            func=AF.Identity, scale=1.0)
                    S.stage("scalar", _em(s_bc2, [kpb], ["M_rep", kpb]))

            def v_w(e):
                return e.tensor_tensor(out=m_w16.rearrange("p a c -> p (a c)"),
                                       in0=m_dl.rearrange("p a c -> p (a c)"),
                                       in1=m_cv.rearrange("p a c -> p (a c)"),
                                       op=ALU.mult)
            S.stage("vector", _em(v_w, ["M_dl", "M_cv"], ["M_w16"]))

            for dt in range(8):
                p = dt % 2
                yb = m_ys[p]
                ky = "M_y%d" % p
                kh = "M_h%d" % p
                hcur = m_h[:, p, :, :]

                def v_u(e, dt=dt):
                    return e.tensor_tensor(
                        out=m_ua,
                        in0=m_w16[:, dt:dt + 1, :].broadcast_to([128, 16, CS]),
                        in1=m_brep, op=ALU.mult)
                S.stage("vector", _em(v_u, ["M_w16", "M_rep"], ["M_ua"]))

                for n in range(NS):
                    da = m_dAs[n % 2]
                    kda = "M_dA%d" % (n % 2)
                    cidx = NS * dt + n

                    def s_da(e, dt=dt, n=n, da=da, cidx=cidx):
                        return e.activation(out=da, in_=m_dl[:, dt, :],
                                            func=AF.Exp,
                                            scale=gA1[:, cidx:cidx + 1])
                    S.stage("scalar", _em(s_da, ["M_dl", "gA1"], [kda]))

                    def v_sc(e, dt=dt, n=n, da=da, cidx=cidx, hcur=hcur):
                        hdst = hcur[:, n, :] if n < NPOOL1 \
                            else m_hv[:, n - NPOOL1, :]
                        return e.tensor_tensor_scan(
                            out=hdst, data0=da, data1=m_ua[:, n, :],
                            initial=gst1[:, cidx:cidx + 1],
                            op0=ALU.mult, op1=ALU.add)
                    S.stage("vector", _em(v_sc, ["M_ua", kda, "gst1", kh],
                                          [kh, "gst1"]))

                def v_fin(e, dt=dt, yb=yb, hcur=hcur):
                    e.tensor_copy(
                        out=gst1[:, NS * dt:NS * dt + NPOOL1]
                        .rearrange("p (a c) -> p a c", c=1),
                        in_=hcur[:, :, CS - 1:CS])
                    e.tensor_copy(
                        out=gst1[:, NS * dt + NPOOL1:NS * dt + NS]
                        .rearrange("p (a c) -> p a c", c=1),
                        in_=m_hv[:, :, CS - 1:CS])
                    e.tensor_tensor(out=m_hv, in0=m_hv,
                                    in1=m_crep[:, NPOOL1:16, :], op=ALU.mult)
                    zf = m_hv.rearrange("p a c -> p (a c)")
                    e.tensor_tensor(out=zf[:, 0:3 * CS], in0=zf[:, 0:3 * CS],
                                    in1=zf[:, 3 * CS:6 * CS], op=ALU.add)
                    e.tensor_tensor(out=zf[:, 0:CS], in0=zf[:, 0:CS],
                                    in1=zf[:, CS:2 * CS], op=ALU.add)
                    e.tensor_tensor(out=yb, in0=zf[:, 0:CS],
                                    in1=zf[:, 2 * CS:3 * CS], op=ALU.add)
                    return e.scalar_tensor_tensor(
                        out=yb, in0=m_cv[:, dt, :],
                        scalar=c_dp[:, dt:dt + 1], in1=yb,
                        op0=ALU.mult, op1=ALU.add)
                S.stage("vector", _em(v_fin, [kh, "M_rep", "M_cv", "gv",
                                              "gst1", ky, "M_sp"],
                                      [ky, "gst1", "M_sp"]))

                def g_con(e, dt=dt, yb=yb, hcur=hcur):
                    e.tensor_tensor(out=hcur, in0=hcur,
                                    in1=m_crep[:, 0:NPOOL1, :], op=ALU.mult)
                    hf = hcur.rearrange("p a c -> p (a c)")
                    e.tensor_tensor(out=hf[:, 0:4 * CS], in0=hf[:, 0:4 * CS],
                                    in1=hf[:, 4 * CS:8 * CS], op=ALU.add)
                    e.tensor_tensor(out=hf[:, 0:2 * CS], in0=hf[:, 0:2 * CS],
                                    in1=hf[:, 2 * CS:4 * CS], op=ALU.add)
                    e.tensor_tensor(out=hf[:, 0:CS], in0=hf[:, 0:CS],
                                    in1=hf[:, CS:2 * CS], op=ALU.add)
                    e.tensor_tensor(out=hf[:, 8 * CS:9 * CS],
                                    in0=hf[:, 8 * CS:9 * CS],
                                    in1=hf[:, 9 * CS:10 * CS], op=ALU.add)
                    e.tensor_tensor(out=hf[:, 0:CS], in0=hf[:, 0:CS],
                                    in1=hf[:, 8 * CS:9 * CS], op=ALU.add)
                    return e.tensor_tensor(out=yb, in0=yb, in1=hf[:, 0:CS],
                                           op=ALU.add)
                S.stage("gpsimd", _em(g_con, [kh, ky, "M_rep"], [ky, kh]))
                dma_stage("scalar", [(sc_y1[:, dt, sl], yb)], [ky], ["sc_y1"])

        S.run(blk)

    with ExitStack() as _stk:
        h_nw2 = _stk.enter_context(nc.sbuf_tensor("n_w2", [128, 8, D], BF16))
        h_ny = _stk.enter_context(nc.sbuf_tensor("n_y", [128, 2, 8, C], F32))
        h_nv = _stk.enter_context(nc.sbuf_tensor("n_v16", [128, 2, 8, C], BF16))
        h_nyg = _stk.enter_context(nc.sbuf_tensor("n_yg", [128, 8, C], BF16))
        h_nrb = _stk.enter_context(nc.sbuf_tensor("n_rb", [128, 2, 8, C], F32))
        h_ns3 = _stk.enter_context(nc.sbuf_tensor("n_s3", [128, 8, C], F32))
        h_nps0 = _stk.enter_context(nc.psum_tensor("n_ps0", [128, C], F32))
        h_nps1 = _stk.enter_context(nc.psum_tensor("n_ps1", [128, C], F32))
        blk = _stk.enter_context(nc.Block())
        n_w2 = h_nw2.ap(); n_y = h_ny.ap(); n_v16 = h_nv.ap()
        n_yg = h_nyg.ap(); n_rb = h_nrb.ap(); n_s3 = h_ns3.ap()
        n_pss = [h_nps0.ap(), h_nps1.ap()]

        dma_stage("gpsimd", [(n_w2, l1_w2.rearrange("(a b) m -> b a m", b=128))],
                  [], ["n_w2"])

        for j in range(nch):
            sl = slice(j * C, (j + 1) * C)
            jp = j % 2
            ny_j = n_y[:, jp, :, :]
            nv_j = n_v16[:, jp, :, :]
            nrb_j = n_rb[:, jp, :, :]
            kny = "N_y%d" % jp
            knrb = "N_rb%d" % jp
            dma_stage("sync", [
                (ny_j, sc_y1[:, :, sl]),
                (nv_j, sc_v[:, :, sl]),
                (nrb_j, sc_xh1[:, :, sl]),
            ], ["sc_y1", "sc_v", "sc_xh1"], [kny, knrb])

            def v_g(e, ny_j=ny_j, nv_j=nv_j):
                return e.tensor_tensor(out=n_yg, in0=ny_j, in1=nv_j,
                                       op=ALU.mult)
            S.stage("vector", _em(v_g, [kny], ["N_yg"]))

            for m in range(8):
                ps = n_pss[m % 2]
                kps = "N_ps%d" % (m % 2)

                def t_mm(e, m=m, ps=ps):
                    last = None
                    for k in range(8):
                        last = e.matmul(ps, n_w2[:, k, m * 128:(m + 1) * 128],
                                        n_yg[:, k, :], start=(k == 0),
                                        stop=(k == 7))
                    return last
                S.stage("tensor", _em(t_mm, ["N_yg", "n_w2"], [kps]))

                def v_ev(e, m=m, ps=ps, nrb_j=nrb_j):
                    return e.scalar_tensor_tensor(out=n_s3[:, m, :], in0=ps,
                                                  scalar=c_w2b[:, m:m + 1],
                                                  in1=nrb_j[:, m, :],
                                                  op0=ALU.add, op1=ALU.add)
                S.stage("vector", _em(v_ev, [kps, knrb, "gv"], ["N_s3"]))

            dma_stage("scalar", [(sc_s3[:, :, sl], n_s3)], ["N_s3"], ["sc_s3"])

        S.run(blk)

    with ExitStack() as _stk:
        h_of1 = _stk.enter_context(nc.sbuf_tensor("o_f1", [128, 8, 3 * D], BF16))
        h_of2 = _stk.enter_context(nc.sbuf_tensor("o_f2", [128, 24, D], BF16))
        h_ox = _stk.enter_context(nc.sbuf_tensor("o_x", [128, 8, C], F32))
        h_osq = _stk.enter_context(nc.sbuf_tensor("o_sq", [128, 8, C], F32))
        h_oxn = _stk.enter_context(nc.sbuf_tensor("o_xn16", [128, 8, C], BF16))
        h_orows = _stk.enter_context(nc.sbuf_tensor("o_rows", [1, 2 * C], F32))
        h_omu = _stk.enter_context(nc.sbuf_tensor("o_mu", [1, C], F32))
        h_ovp = _stk.enter_context(nc.sbuf_tensor("o_vp", [1, C], F32))
        h_oq = _stk.enter_context(nc.sbuf_tensor("o_q", [1, C], F32))
        h_orep = _stk.enter_context(nc.sbuf_tensor("o_rep", [128, 2 * C], F32))
        h_oh16 = _stk.enter_context(nc.sbuf_tensor("o_h16", [128, 24, C], BF16))
        h_os2 = _stk.enter_context(nc.sbuf_tensor("o_s2", [128, 8, C], F32))
        h_osta = _stk.enter_context(nc.psum_tensor("o_sta", [1, C], F32))
        h_ostb = _stk.enter_context(nc.psum_tensor("o_stb", [1, C], F32))
        h_ops0 = _stk.enter_context(nc.psum_tensor("o_ps0", [128, C], F32))
        h_ops1 = _stk.enter_context(nc.psum_tensor("o_ps1", [128, C], F32))
        blk = _stk.enter_context(nc.Block())
        o_f1 = h_of1.ap(); o_f2 = h_of2.ap(); o_x = h_ox.ap()
        o_sq = h_osq.ap(); o_xh = o_sq; o_xn16 = h_oxn.ap()
        o_rows = h_orows.ap(); o_mu = h_omu.ap(); o_vp = h_ovp.ap()
        o_q = h_oq.ap(); o_f = h_ostb.ap(); o_rep = h_orep.ap()
        o_h16 = h_oh16.ap(); o_s2 = h_os2.ap(); o_s4 = o_sq
        o_pss = [h_ops0.ap(), h_ops1.ap()]
        lnbo = {"tag": "lnO", "sq": o_sq, "sta": h_osta.ap(), "stb": h_ostb.ap(),
                "mu": o_mu, "vp": o_vp, "q": o_q, "f": o_f,
                "rows": o_rows, "rep": o_rep}

        dma_stage("gpsimd", [
            (o_f1, l1_ff1.rearrange("(a b) m -> b a m", b=128)),
            (o_f2, l1_ff2.rearrange("(a b) m -> b a m", b=128)),
        ], [], ["o_f1", "o_f2"])

        for j in range(nch):
            sl = slice(j * C, (j + 1) * C)
            dma_stage("sync", [(o_x, sc_s3[:, :, sl]),
                               (o_s2, sc_s2[:, :, sl])],
                      ["sc_s3", "sc_s2"], ["O_x", "O_s2"])
            emit_ln("O", o_x, 8, lnbo, second=False)
            emit_norm("O", o_x, o_rep, o_xh, 8, "lnO", xh_key="lnO_sq")

            def s_xn(e):
                last = None
                for dt in range(8):
                    last = e.activation(out=o_xn16[:, dt, :],
                                        in_=o_xh[:, dt, :],
                                        func=AF.Identity,
                                        scale=c_ln2g[:, dt:dt + 1],
                                        bias=c_ln2b[:, dt:dt + 1])
                return last
            S.stage("scalar", _em(s_xn, ["lnO_sq", "gv"], ["O_xn"]))

            for m in range(24):
                ps = o_pss[m % 2]
                kps = "O_ps%d" % (m % 2)

                def t_mm(e, m=m, ps=ps):
                    last = None
                    for k in range(8):
                        last = e.matmul(ps, o_f1[:, k, m * 128:(m + 1) * 128],
                                        o_xn16[:, k, :], start=(k == 0),
                                        stop=(k == 7))
                    return last
                S.stage("tensor", _em(t_mm, ["O_xn", "o_f1"], [kps]))

                def s_ev(e, m=m, ps=ps):
                    return e.activation(out=o_h16[:, m, :], in_=ps,
                                        func=AF.Gelu, scale=1.0,
                                        bias=c_ff1b[:, m:m + 1])
                S.stage("scalar", _em(s_ev, [kps, "gv"], ["O_h16"]))

            for m in range(8):
                ps = o_pss[m % 2]
                kps = "O_ps%d" % (m % 2)

                def t_mm(e, m=m, ps=ps):
                    last = None
                    for k in range(24):
                        last = e.matmul(ps, o_f2[:, k, m * 128:(m + 1) * 128],
                                        o_h16[:, k, :], start=(k == 0),
                                        stop=(k == 23))
                    return last
                S.stage("tensor", _em(t_mm, ["O_h16", "o_f2"], [kps]))

                def v_ev(e, m=m, ps=ps):
                    e.scalar_tensor_tensor(out=o_s4[:, m, :], in0=ps,
                                           scalar=c_ff2b[:, m:m + 1],
                                           in1=o_x[:, m, :],
                                           op0=ALU.add, op1=ALU.add)
                    return e.tensor_tensor(out=o_s4[:, m, :],
                                           in0=o_s4[:, m, :],
                                           in1=o_s2[:, m, :], op=ALU.add)
                S.stage("vector", _em(v_ev, [kps, "O_x", "O_s2", "gv"],
                                      ["lnO_sq"]))

            dma_stage("scalar", [(sc_s4[:, :, sl], o_s4)], ["lnO_sq"], ["sc_s4"])

        S.run(blk)

    with ExitStack() as _stk:
        h_zct = _stk.enter_context(nc.sbuf_tensor("z_ct", [128, 8, D], BF16))
        h_zst = _stk.enter_context(nc.sbuf_tensor("z_st", [128, 8, D], BF16))
        h_zwa = _stk.enter_context(nc.sbuf_tensor("z_wa", [128, 8, 512], F32))
        h_zwb = _stk.enter_context(nc.sbuf_tensor("z_wb", [128, 8, 512], F32))
        h_zx = _stk.enter_context(nc.sbuf_tensor("z_x", [128, 2, 8, C], F32))
        h_zsq = _stk.enter_context(nc.sbuf_tensor("z_sq", [128, 8, C], F32))
        h_zxh = _stk.enter_context(nc.sbuf_tensor("z_xh", [128, 8, C], F32))
        h_zx16 = _stk.enter_context(nc.sbuf_tensor("z_x16", [128, 8, C], BF16))
        h_zrows = _stk.enter_context(nc.sbuf_tensor("z_rows", [1, 2 * C], F32))
        h_zmu = _stk.enter_context(nc.sbuf_tensor("z_mu", [1, C], F32))
        h_zvp = _stk.enter_context(nc.sbuf_tensor("z_vp", [1, C], F32))
        h_zq = _stk.enter_context(nc.sbuf_tensor("z_q", [1, C], F32))
        h_zrep = _stk.enter_context(nc.sbuf_tensor("z_rep", [128, 2 * C], F32))
        h_zs = _stk.enter_context(nc.sbuf_tensor("z_s", [128, 8, C], F32))
        h_zt = _stk.enter_context(nc.sbuf_tensor("z_t", [128, 8, C], F32))
        h_zo = _stk.enter_context(nc.sbuf_tensor("z_o", [128, 8, C], F32))
        h_zsta = _stk.enter_context(nc.psum_tensor("z_sta", [1, C], F32))
        h_zstb = _stk.enter_context(nc.psum_tensor("z_stb", [1, C], F32))
        h_zps0 = _stk.enter_context(nc.psum_tensor("z_ps0", [128, C], F32))
        h_zps1 = _stk.enter_context(nc.psum_tensor("z_ps1", [128, C], F32))
        blk = _stk.enter_context(nc.Block())
        z_ct = h_zct.ap(); z_st = h_zst.ap()
        z_wa = h_zwa.ap(); z_wb = h_zwb.ap()
        z_x = h_zx.ap(); z_sq = h_zsq.ap(); z_xh = h_zxh.ap()
        z_x16 = h_zx16.ap(); z_rows = h_zrows.ap()
        z_mu = h_zmu.ap(); z_vp = h_zvp.ap(); z_q = h_zq.ap(); z_f = h_zstb.ap()
        z_rep = h_zrep.ap(); z_s = h_zs.ap(); z_t = h_zt.ap(); z_o = h_zo.ap()
        z_pss = [h_zps0.ap(), h_zps1.ap()]
        lnbz = {"tag": "lnZ", "sq": z_sq, "sta": h_zsta.ap(), "stb": h_zstb.ap(),
                "mu": z_mu, "vp": z_vp, "q": z_q, "f": z_f,
                "rows": z_rows, "rep": z_rep}

        for half in range(2):
            hsl = slice(half * 512, (half + 1) * 512)
            dma_stage("sync", [
                (z_wa, l1f_th.rearrange("(a b) m -> b a m", b=128)[:, :, hsl]),
                (z_wb, l1f_tn.rearrange("(a b) m -> b a m", b=128)[:, :, hsl]),
            ], [], ["Z_wa", "Z_wb"])

            def v_add(e):
                return e.tensor_tensor(out=z_wa.rearrange("p a c -> p (a c)"),
                                       in0=z_wa.rearrange("p a c -> p (a c)"),
                                       in1=z_wb.rearrange("p a c -> p (a c)"),
                                       op=ALU.add)
            S.stage("vector", _em(v_add, ["Z_wa", "Z_wb"], ["Z_wa"]))

            def s_cos(e, hsl=hsl):
                return e.activation(out=z_ct[:, :, hsl], in_=z_wa,
                                    func=AF.Sin, bias=K_HALFPI)
            S.stage("scalar", _em(s_cos, ["Z_wa"], ["z_ct"]))

            dma_stage("sync", [
                (z_wa, l1f_ph.rearrange("(a b) m -> b a m", b=128)[:, :, hsl]),
                (z_wb, l1f_pn.rearrange("(a b) m -> b a m", b=128)[:, :, hsl]),
            ], [], ["Z_wa", "Z_wb"])

            def v_add2(e):
                return e.tensor_tensor(out=z_wa.rearrange("p a c -> p (a c)"),
                                       in0=z_wa.rearrange("p a c -> p (a c)"),
                                       in1=z_wb.rearrange("p a c -> p (a c)"),
                                       op=ALU.add)
            S.stage("vector", _em(v_add2, ["Z_wa", "Z_wb"], ["Z_wa"]))

            def s_sin(e, hsl=hsl):
                return e.activation(out=z_st[:, :, hsl], in_=z_wa,
                                    func=AF.Sin)
            S.stage("scalar", _em(s_sin, ["Z_wa"], ["z_st"]))

        for j in range(nch):
            sl = slice(j * C, (j + 1) * C)
            jp = j % 2
            zx_j = z_x[:, jp, :, :]
            kzx = "Z_x%d" % jp
            dma_stage("sync", [(zx_j, sc_s4[:, :, sl])], ["sc_s4"], [kzx])
            emit_ln("Z", zx_j, 8, lnbz, second=False, xkey=kzx)
            emit_norm("Z", zx_j, z_rep, z_xh, 8, "lnZ", xkey=kzx)

            def v_c16(e):
                return e.tensor_copy(out=z_x16.rearrange("p a c -> p (a c)"),
                                     in_=z_xh.rearrange("p a c -> p (a c)"))
            S.stage("vector", _em(v_c16, ["Z_xh"], ["Z_x16"]))

            for m in range(8):
                def t_re(e, m=m):
                    last = None
                    for k in range(8):
                        last = e.matmul(z_pss[0], z_ct[:, k, m * 128:(m + 1) * 128],
                                        z_x16[:, k, :], start=(k == 0),
                                        stop=(k == 7))
                    return last
                S.stage("tensor", _em(t_re, ["Z_x16", "z_ct"], ["Z_ps0"]))

                def t_im(e, m=m):
                    last = None
                    for k in range(8):
                        last = e.matmul(z_pss[1], z_st[:, k, m * 128:(m + 1) * 128],
                                        z_x16[:, k, :], start=(k == 0),
                                        stop=(k == 7))
                    return last
                S.stage("tensor", _em(t_im, ["Z_x16", "z_st"], ["Z_ps1"]))

                def s_sq0(e, m=m):
                    return e.activation(out=z_s[:, m, :], in_=z_pss[0],
                                        func=AF.Square)
                S.stage("scalar", _em(s_sq0, ["Z_ps0"], ["Z_s"]))

                def s_sq1(e, m=m):
                    return e.activation(out=z_t[:, m, :], in_=z_pss[1],
                                        func=AF.Square)
                S.stage("scalar", _em(s_sq1, ["Z_ps1"], ["Z_t"]))

                def v_mag(e, m=m):
                    e.tensor_tensor(out=z_s[:, m, :], in0=z_s[:, m, :],
                                    in1=z_t[:, m, :], op=ALU.add)
                    return e.tensor_scalar_max(out=z_s[:, m, :],
                                               in0=z_s[:, m, :],
                                               scalar1=1e-30)
                S.stage("vector", _em(v_mag, ["Z_s", "Z_t"], ["Z_s"]))

            def s_sqrt(e):
                return e.activation(out=z_t.rearrange("p a c -> p (a c)"),
                                    in_=z_s.rearrange("p a c -> p (a c)"),
                                    func=AF.Sqrt)
            S.stage("scalar", _em(s_sqrt, ["Z_s"], ["Z_t"]))

            def v_nt(e, zx_j=zx_j):
                fs = z_s.rearrange("p a c -> p (a c)")
                fq0 = z_t.rearrange("p a c -> p (a c)")
                fo = z_o.rearrange("p a c -> p (a c)")
                fx = zx_j.rearrange("p a c -> p (a c)")
                e.reciprocal(out=fo, in_=fq0)
                e.tensor_tensor(out=fo, in0=fo, in1=fs, op=ALU.mult)
                e.tensor_tensor(out=fo, in0=fo, in1=fq0, op=ALU.add)
                e.tensor_scalar_mul(out=fo, in0=fo, scalar1=0.5)
                return e.tensor_tensor(out=fo, in0=fo, in1=fx, op=ALU.add)
            S.stage("vector", _em(v_nt, ["Z_t", "Z_s", kzx], ["Z_o"]))

            dma_stage("scalar", [(out[:, :, sl], z_o)], ["Z_o"], ["out"])

        S.run(blk)

    return nc



def _vec_tiles(v, ntile):
    return np.ascontiguousarray(np.asarray(v, dtype=np.float32)
                                .reshape(ntile, 128).T)


def _fm(x):
    x = np.asarray(x, dtype=np.float32)
    Tt, F = x.shape
    return np.ascontiguousarray(x.T.reshape(F // 128, 128, Tt)
                                .transpose(1, 0, 2))


def _dropout_maskval(shape):
    import jax
    with jax.default_device(jax.devices("cpu")[0]):
        keep = jax.random.bernoulli(
            jax.random.key(42, impl="threefry2x32"), 0.8, shape)
        keep = np.asarray(keep)
    return np.where(keep, np.float32(1.25), np.float32(0.0))


def prep_inmaps(inputs, T):
    i = {k: np.asarray(v) for k, v in inputs.items()}
    B = i["L0_attn_x"].shape[0]
    mv = _dropout_maskval((B, T, D))
    base = {
        "l0_lnw": _vec_tiles(i["L0_attn_ln_w"], 8),
        "l0_lnb": _vec_tiles(i["L0_attn_ln_b"], 8),
        "l0_win": np.ascontiguousarray(i["L0_attn_W_in"], dtype=np.float32),
        "l0_bin": _vec_tiles(i["L0_attn_b_in"], 32),
        "l0_convw": np.ascontiguousarray(
            i["L0_attn_conv_w"].reshape(16, 128, 4).transpose(1, 0, 2)
            if i["L0_attn_conv_w"].ndim == 2 else
            i["L0_attn_conv_w"][:, 0, :].reshape(16, 128, 4).transpose(1, 0, 2),
            dtype=np.float32),
        "l0_convb": _vec_tiles(i["L0_attn_conv_b"], 16),
        "l0_alog": np.ascontiguousarray(
            i["L0_attn_A_log"].reshape(16, 128, NS).transpose(1, 0, 2),
            dtype=np.float32),
        "l0_d": _vec_tiles(i["L0_attn_D"], 16),
        "l0_wx": np.ascontiguousarray(i["L0_attn_W_x"], dtype=np.float32),
        "l0_bx": np.ascontiguousarray(i["L0_attn_b_x"].reshape(32, 1),
                                      dtype=np.float32),
        "l0_wout": np.ascontiguousarray(i["L0_attn_W_out"], dtype=np.float32),
        "l0_bout": _vec_tiles(i["L0_attn_b_out"], 8),
        "f0_w1t": np.ascontiguousarray(i["L0_ffn_W1"].T, dtype=np.float32),
        "f0_b1": _vec_tiles(i["L0_ffn_b1"], 32),
        "f0_w2t": np.ascontiguousarray(i["L0_ffn_W2"].T, dtype=np.float32),
        "f0_b2": _vec_tiles(i["L0_ffn_b2"], 8),
        "l1_ln1g": _vec_tiles(i["L1_attn_ln1_g"], 8),
        "l1_ln1b": _vec_tiles(i["L1_attn_ln1_b"], 8),
        "l1_lng": _vec_tiles(i["L1_attn_ln_g"], 8),
        "l1_lnb": _vec_tiles(i["L1_attn_ln_b"], 8),
        "l1_w1": np.ascontiguousarray(i["L1_attn_w1_w"], dtype=np.float32),
        "l1_w1b": _vec_tiles(i["L1_attn_w1_b"], 8),
        "l1_v1": np.ascontiguousarray(i["L1_attn_v1_w"], dtype=np.float32),
        "l1_v1b": _vec_tiles(i["L1_attn_v1_b"], 8),
        "l1_w2": np.ascontiguousarray(i["L1_attn_w2_w"], dtype=np.float32),
        "l1_w2b": _vec_tiles(i["L1_attn_w2_b"], 8),
        "l1_convw": np.ascontiguousarray(
            i["L1_attn_conv_w"][:, 0, :].reshape(8, 128, 3).transpose(1, 0, 2),
            dtype=np.float32),
        "l1_dtw": np.ascontiguousarray(i["L1_attn_dt_w"], dtype=np.float32),
        "l1_dtb": _vec_tiles(i["L1_attn_dt_b"], 8),
        "l1_bpw": np.ascontiguousarray(i["L1_attn_Bp_w"], dtype=np.float32),
        "l1_bpb": np.ascontiguousarray(i["L1_attn_Bp_b"].reshape(16, 1),
                                       dtype=np.float32),
        "l1_cpw": np.ascontiguousarray(i["L1_attn_Cp_w"], dtype=np.float32),
        "l1_cpb": np.ascontiguousarray(i["L1_attn_Cp_b"].reshape(16, 1),
                                       dtype=np.float32),
        "l1_alog": np.ascontiguousarray(
            i["L1_attn_A_log"].reshape(8, 128, NS).transpose(1, 0, 2),
            dtype=np.float32),
        "l1_dp": _vec_tiles(i["L1_attn_D_p"], 8),
        "l1_ln2g": _vec_tiles(i["L1_attn_ln2_g"], 8),
        "l1_ln2b": _vec_tiles(i["L1_attn_ln2_b"], 8),
        "l1_ff1": np.ascontiguousarray(i["L1_attn_ff1_w"], dtype=np.float32),
        "l1_ff1b": _vec_tiles(i["L1_attn_ff1_b"], 24),
        "l1_ff2": np.ascontiguousarray(i["L1_attn_ff2_w"], dtype=np.float32),
        "l1_ff2b": _vec_tiles(i["L1_attn_ff2_b"], 8),
        "l1f_th": np.ascontiguousarray(i["L1_ffn_theta"], dtype=np.float32),
        "l1f_ph": np.ascontiguousarray(i["L1_ffn_phi"], dtype=np.float32),
        "l1f_tn": np.ascontiguousarray(i["L1_ffn_theta_noise"], dtype=np.float32),
        "l1f_pn": np.ascontiguousarray(i["L1_ffn_phi_noise"], dtype=np.float32),
    }
    maps = []
    for core in range(8):
        b = (core // 2) % B
        m = dict(base)
        m["xin"] = _fm(i["L0_attn_x"][b])
        m["mask16"] = _fm(mv[b]).astype(ml_dtypes.bfloat16)
        m["bceye"] = np.ascontiguousarray(
            np.repeat(np.eye(16, dtype=np.float32), 128, axis=1)
        ).astype(ml_dtypes.bfloat16)
        maps.append(m)
    return maps


def kernel(**inputs):
    T = inputs["L0_attn_x"].shape[1]
    key = ("nc", T)
    if key not in _NC_CACHE:
        _NC_CACHE[key] = build_nc(T=T)
    nc = _NC_CACHE[key]
    maps = prep_inmaps(inputs, T)
    res = run_bass_kernel_spmd(nc, maps, core_ids=list(range(8)))
    B = inputs["L0_attn_x"].shape[0]
    outs = []
    for b in range(B):
        o = np.asarray(res.results[2 * b]["out"])
        outs.append(o.transpose(1, 0, 2).reshape(D, T).T)
    return np.stack(outs).astype(np.float32)


# version 21
# speedup vs baseline: 1.0067x; 1.0067x over previous
import math
from contextlib import ExitStack
import numpy as np
import ml_dtypes
import concourse.bass as bass
import concourse.mybir as mybir
from concourse.bass_utils import run_bass_kernel_spmd

F32 = mybir.dt.float32
BF16 = mybir.dt.bfloat16
AF = mybir.ActivationFunctionType
ALU = mybir.AluOpType

D = 1024
DI = 2048
NS = 16
LN2C = math.log(2.0)
EPS = 1e-5
HALFPI = math.pi / 2.0

ENGS = ("sync", "scalar", "vector", "tensor", "gpsimd")
_NC_CACHE = {}
DMAE = ("sync", "scalar", "gpsimd")


_NO_CHAIN = frozenset((
    "wait_ge", "wait_op", "sem_inc", "sem_clear", "maybe_wait_ge",
    "maybe_drain_then_inc", "maybe_nop_then_inc", "dma_start",
    "dma_start_transpose", "nop", "drain", "engine_nop",
))


class _ChainEng:

    def __init__(self, e, sem, ctr):
        self._e = e
        self._sem = sem
        self._ctr = ctr

    def __getattr__(self, name):
        f = getattr(self._e, name)
        if name in _NO_CHAIN or not callable(f):
            return f

        def wrap(*a, **k):
            inst = f(*a, **k)
            if hasattr(inst, "then_inc"):
                if self._ctr[0] > 0:
                    inst._wait_ge(self._sem, self._ctr[0])
                self._ctr[0] += 1
                inst.then_inc(self._sem, 1)
            return inst
        return wrap


class Sched:

    NSLOT = {"sync": 10, "gpsimd": 6, "scalar": 4}

    def __init__(self, nc):
        self.nc = nc
        self.exec_sem = {e: nc.alloc_semaphore("x_" + e) for e in ENGS}
        self.exec_val = {e: 0 for e in ENGS}
        self.chain_sem = {e: nc.alloc_semaphore("c_" + e) for e in ENGS}
        self.chain_ctr = {e: [0] for e in ENGS}
        self.pool_sems = []
        self.pool_val = []
        self.pool_base = {}
        self.pool_rr = {}
        for e in DMAE:
            self.pool_base[e] = len(self.pool_sems)
            self.pool_rr[e] = 0
            for i in range(self.NSLOT[e]):
                self.pool_sems.append(nc.alloc_semaphore(f"dp_{e}{i}"))
                self.pool_val.append(0)
        self.bar_sem = nc.alloc_semaphore("bar")
        self.bar_val = 0
        self.stages = []
        self.last_w = {}
        self.readers = {}

    def _deps_for(self, reads, writes):
        deps = {}

        def need(tok):
            for key, v in tok:
                deps[key] = max(deps.get(key, 0), v)

        for b in reads:
            if b in self.last_w:
                need(self.last_w[b])
        for b in writes:
            if b in self.last_w:
                need(self.last_w[b])
            for tok in self.readers.get(b, ()):
                need(tok)
        return deps

    def _commit(self, reads, writes, tok):
        for b in reads:
            self.readers.setdefault(b, []).append(tok)
        for b in writes:
            self.last_w[b] = tok
            self.readers[b] = []

    def stage(self, eng, emit, dma=False):
        assert not dma
        deps = self._deps_for(emit.reads, emit.writes)
        self.exec_val[eng] += 1
        tok = [(("exec", eng), self.exec_val[eng])]
        self._commit(emit.reads, emit.writes, tok)
        self.stages.append((eng, emit, deps, False, None))

    def dma(self, eng, dmas, reads, writes):
        deps = self._deps_for(reads, writes)
        plan = []
        for _ in dmas:
            gid = self.pool_base[eng] + self.pool_rr[eng]
            self.pool_rr[eng] = (self.pool_rr[eng] + 1) % self.NSLOT[eng]
            prev = self.pool_val[gid]
            self.pool_val[gid] = prev + 16
            plan.append((gid, prev, prev + 16))
        tok = [(("dmap", gid), after) for (gid, prev, after) in plan]

        def emit(e, dmas=tuple(dmas), plan=tuple(plan), sched=self):
            for (dst, src), (gid, prev, after) in zip(dmas, plan):
                inst = e.dma_start(out=dst, in_=src)
                if prev > 0:
                    inst._wait_ge(sched.pool_sems[gid], prev)
                inst.then_inc(sched.pool_sems[gid], 16)
            return None
        emit.reads = tuple(reads)
        emit.writes = tuple(writes)
        self._commit(reads, writes, tok)
        self.stages.append((eng, emit, deps, True, plan))

    def run(self, blk):
        bodies = {e: [s for s in self.stages if s[0] == e] for e in ENGS}
        sched = self
        tensor_val_now = self.exec_val["tensor"]
        pool_now = list(self.pool_val)
        self.bar_val += 4

        def make_body(ename):
            def body(e):
                bound = {}
                ep = (_ChainEng(e, sched.chain_sem[ename],
                                sched.chain_ctr[ename])
                      if ename in ("vector", "scalar", "gpsimd") else e)
                for (eng, emit, deps, dma, plan) in bodies[ename]:
                    for key in sorted(deps, key=repr):
                        v = deps[key]
                        if key[0] == "exec" and key[1] == ename:
                            continue
                        if bound.get(key, 0) >= v:
                            continue
                        sem = (sched.exec_sem[key[1]] if key[0] == "exec"
                               else sched.pool_sems[key[1]])
                        e.wait_ge(sem, v)
                        bound[key] = v
                    last = emit(ep)
                    if not dma:
                        if ename in ("vector", "scalar", "gpsimd"):
                            ev = e.sem_inc(sched.exec_sem[ename], 1)
                            cc = sched.chain_ctr[ename][0]
                            if cc > 0:
                                ev._wait_ge(sched.chain_sem[ename], cc)
                        elif last is not None:
                            last.then_inc(sched.exec_sem[ename], 1)
                        else:
                            e.sem_inc(sched.exec_sem[ename], 1)
                if ename == "tensor":
                    e.wait_ge(sched.bar_sem, sched.bar_val)
                else:
                    if ename in DMAE:
                        base = sched.pool_base[ename]
                        for i in range(sched.NSLOT[ename]):
                            if pool_now[base + i] > 0:
                                e.wait_ge(sched.pool_sems[base + i],
                                          pool_now[base + i])
                    if tensor_val_now > 0:
                        e.wait_ge(sched.exec_sem["tensor"], tensor_val_now)
                    e.sem_inc(sched.bar_sem, 1)
                    e.wait_ge(sched.bar_sem, sched.bar_val)
            return body

        blk.sync(make_body("sync"))
        blk.scalar(make_body("scalar"))
        blk.vector(make_body("vector"))
        blk.tensor(make_body("tensor"))
        blk.gpsimd(make_body("gpsimd"))
        self.stages = []


def _em(fn, reads, writes, n_dma=None):
    fn.reads = tuple(reads)
    fn.writes = tuple(writes)
    if n_dma is not None:
        fn.n_dma = n_dma
    return fn



def build_nc(T=8192, C=512, CS=512, CF=512):
    nch = T // C
    nsc = T // CS
    ncf = T // CF
    nc = bass.Bass("TRN2", target_bir_lowering=False)
    S = Sched(nc)

    def din(name, shape, dt=F32):
        return nc.dram_tensor(name, shape, dt, kind="ExternalInput").ap()

    def scr(name, shape, dt=F32):
        return nc.dram_tensor(name, shape, dt, kind="Internal").ap()

    xin = din("xin", [128, 8, T])
    mask16 = din("mask16", [128, 8, T], BF16)
    l0_lnw = din("l0_lnw", [128, 8]); l0_lnb = din("l0_lnb", [128, 8])
    l0_win = din("l0_win", [D, 2 * DI]); l0_bin = din("l0_bin", [128, 32])
    l0_convw = din("l0_convw", [128, 16, 4]); l0_convb = din("l0_convb", [128, 16])
    l0_alog = din("l0_alog", [128, 16, NS]); l0_d = din("l0_d", [128, 16])
    l0_wx = din("l0_wx", [DI, 2 * NS]); l0_bx = din("l0_bx", [32, 1])
    l0_wout = din("l0_wout", [DI, D]); l0_bout = din("l0_bout", [128, 8])
    f0_w1t = din("f0_w1t", [D, 4 * D]); f0_b1 = din("f0_b1", [128, 32])
    f0_w2t = din("f0_w2t", [4 * D, D]); f0_b2 = din("f0_b2", [128, 8])
    l1_ln1g = din("l1_ln1g", [128, 8]); l1_ln1b = din("l1_ln1b", [128, 8])
    l1_lng = din("l1_lng", [128, 8]); l1_lnb = din("l1_lnb", [128, 8])
    l1_w1 = din("l1_w1", [D, D]); l1_w1b = din("l1_w1b", [128, 8])
    l1_v1 = din("l1_v1", [D, D]); l1_v1b = din("l1_v1b", [128, 8])
    l1_w2 = din("l1_w2", [D, D]); l1_w2b = din("l1_w2b", [128, 8])
    l1_convw = din("l1_convw", [128, 8, 3])
    l1_dtw = din("l1_dtw", [D, D]); l1_dtb = din("l1_dtb", [128, 8])
    l1_bpw = din("l1_bpw", [D, NS]); l1_bpb = din("l1_bpb", [16, 1])
    l1_cpw = din("l1_cpw", [D, NS]); l1_cpb = din("l1_cpb", [16, 1])
    l1_alog = din("l1_alog", [128, 8, NS]); l1_dp = din("l1_dp", [128, 8])
    l1_ln2g = din("l1_ln2g", [128, 8]); l1_ln2b = din("l1_ln2b", [128, 8])
    l1_ff1 = din("l1_ff1", [D, 3 * D]); l1_ff1b = din("l1_ff1b", [128, 24])
    l1_ff2 = din("l1_ff2", [3 * D, D]); l1_ff2b = din("l1_ff2b", [128, 8])
    bceye = din("bceye", [16, 16 * 128], BF16)
    l1f_th = din("l1f_th", [D, D]); l1f_ph = din("l1f_ph", [D, D])
    l1f_tn = din("l1f_tn", [D, D]); l1f_pn = din("l1f_pn", [D, D])
    out = nc.dram_tensor("out", [128, 8, T], F32, kind="ExternalOutput").ap()

    sc_xconv = scr("sc_xconv", [128, 16, T], BF16)
    sc_gate = scr("sc_gate", [128, 16, T], BF16)
    sc_b0 = scr("sc_b0", [32, T], BF16)
    sc_y0 = scr("sc_y0", [128, 16, T])
    sc_r0 = scr("sc_r0", [128, 8, T])
    sc_s1 = scr("sc_s1", [128, 8, T])
    sc_hq = scr("sc_hq", [128, 32, T], BF16)
    sc_w2q = scr("sc_w2q", [4 * D, D], BF16)
    sc_s2 = scr("sc_s2", [128, 8, T])
    sc_xh1 = scr("sc_xh1", [128, 8, T])
    sc_x1 = scr("sc_x1", [128, 8, T])
    sc_delta = scr("sc_delta", [128, 8, T])
    sc_bc1 = scr("sc_bc1", [16, 2 * T], BF16)
    sc_v = scr("sc_v", [128, 8, T], BF16)
    sc_y1 = scr("sc_y1", [128, 8, T])
    sc_s3 = scr("sc_s3", [128, 8, T])
    sc_s4 = scr("sc_s4", [128, 8, T])
    sc_row = scr("sc_row", [1, 2 * C])

    def persist(name, shape, dt=F32):
        return nc.alloc_sbuf_tensor(name, shape, dt).ap()

    gv = persist("gv", [128, 304])
    off = [0]

    def col(n):
        a = gv[:, off[0]:off[0] + n]
        off[0] += n
        return a

    c_l0lnw = col(8); c_l0lnb = col(8)
    c_bin = col(32); c_convb = col(16); c_d0 = col(16)
    c_bout = col(8); c_b1q = col(32); c_b2q = col(8)
    c_ln1g = col(8); c_ln1b = col(8); c_lng = col(8); c_lnb = col(8)
    c_w1b = col(8); c_v1b = col(8); c_w2b = col(8)
    c_dtb = col(8); c_dp = col(8); c_ln2g = col(8); c_ln2b = col(8)
    c_ff1b = col(24); c_ff2b = col(8)
    c_b1r = col(32); c_b2r = col(8)
    assert off[0] <= 304

    ga0 = persist("ga0", [128, 16 * NS])
    gst0 = persist("gst0", [128, 16 * NS])
    gA1 = persist("gA1", [128, 8 * NS])
    gst1 = persist("gst1", [128, 8 * NS])
    gcw0 = persist("gcw0", [128, 16 * 4])
    gcw1 = persist("gcw1", [128, 8 * 3])
    gbx = persist("gbx", [32, 4])
    gones = persist("gones", [128, 1])
    gonesr = persist("gonesr", [1, 128])
    gfqt = persist("gfqt", [128, 2 * 32])
    gk = persist("gk", [128, 8])
    K_NEG1 = gk[:, 0:1]; K_NLN2 = gk[:, 1:2]; K_2LN2 = gk[:, 2:3]
    K_LN2 = gk[:, 3:4]; K_N2LN2 = gk[:, 4:5]; K_HALFPI = gk[:, 5:6]
    K_ONE = gk[:, 6:7]

    def emit_fq(src_key, src_ap, dst_ap, tmp_ap, signed, dst_key, tmp_key,
                ties_even=False):
        shp = list(src_ap.shape)
        W = 1
        for s in shp[1:]:
            W *= s

        def tview(k):
            t = tmp_ap[:, k * W:(k + 1) * W]
            if len(shp) == 3:
                t = t.rearrange("p (a c) -> p a c", c=shp[2])
            return t
        t_m, t_t = tview(0), tview(1)
        tk = tmp_key
        CQ = float(2 ** 21 + 1)

        def v_q(e):
            if signed:
                e.tensor_scalar_mul(out=t_m, in0=src_ap, scalar1=CQ)
                e.tensor_tensor(out=t_t, in0=t_m, in1=src_ap, op=ALU.subtract)
            else:
                e.tensor_scalar(out=dst_ap, in0=src_ap, scalar1=0.0,
                                scalar2=None, op0=ALU.max)
                e.tensor_scalar_mul(out=t_m, in0=dst_ap, scalar1=CQ)
                e.tensor_tensor(out=t_t, in0=t_m, in1=dst_ap, op=ALU.subtract)
            return e.tensor_tensor(out=dst_ap, in0=t_m, in1=t_t,
                                   op=ALU.subtract)
        S.stage("vector", _em(v_q, [src_key], [dst_key, tk]))

    def emit_ln(name, x_ap, ntile, bufs, second=False, xkey=None):
        sq = bufs["sq"]; sta = bufs["sta"]; stb = bufs["stb"]
        mu = bufs["mu"]; vp = bufs["vp"]; q = bufs["q"]; f = bufs["f"]
        rows = bufs["rows"]; rep = bufs["rep"]; tag = bufs["tag"]
        kx = xkey or (name + "_x")
        ksq, kst, krow, krep = (tag + s for s in
                                ("_sq", "_st", "_rows", "_rep"))
        Cw = x_ap.shape[2]

        def s_sq(e):
            return e.activation(out=sq.rearrange("p a c -> p (a c)"),
                                in_=x_ap.rearrange("p a c -> p (a c)"),
                                func=AF.Square)
        S.stage("scalar", _em(s_sq, [kx], [ksq]))

        def t_st(e):
            for dt in range(ntile):
                e.matmul(sta, gones, x_ap[:, dt, :],
                         start=(dt == 0), stop=(dt == ntile - 1))
            last = None
            for dt in range(ntile):
                last = e.matmul(stb, gones, sq[:, dt, :],
                                start=(dt == 0), stop=(dt == ntile - 1))
            return last
        S.stage("tensor", _em(t_st, [kx, ksq, "gones"], [kst, krow]))

        def v_r1(e):
            e.tensor_scalar_mul(out=mu, in0=sta, scalar1=1.0 / 1024.0)
            e.tensor_scalar(out=vp, in0=stb, scalar1=1.0 / 1024.0, scalar2=EPS,
                            op0=ALU.mult, op1=ALU.add)
            e.tensor_tensor(out=q, in0=mu, in1=mu, op=ALU.mult)
            return e.tensor_tensor(out=vp, in0=vp, in1=q, op=ALU.subtract)
        S.stage("vector", _em(v_r1, [kst], [krow]))

        def s_r1(e):
            return e.activation(out=q, in_=vp, func=AF.Sqrt)
        S.stage("scalar", _em(s_r1, [krow], [krow]))

        def v_r2(e):
            inv = rows[0:1, 0:Cw]
            t = rows[0:1, Cw:2 * Cw]
            e.reciprocal(out=inv, in_=q)
            e.tensor_tensor(out=t, in0=inv, in1=inv, op=ALU.mult)
            e.tensor_tensor(out=t, in0=t, in1=vp, op=ALU.mult)
            e.tensor_scalar(out=t, in0=t, scalar1=-0.5, scalar2=1.5,
                            op0=ALU.mult, op1=ALU.add)
            last = e.tensor_tensor(out=inv, in0=inv, in1=t, op=ALU.mult)
            if second:
                e.reciprocal(out=f, in_=vp)
                last = e.tensor_scalar(out=f, in0=f, scalar1=-EPS,
                                       scalar2=1.0 + EPS,
                                       op0=ALU.mult, op1=ALU.add)
            return last
        S.stage("vector", _em(v_r2, [krow], [krow]))

        if second:
            def s_r2(e):
                return e.activation(out=q, in_=f, func=AF.Sqrt)
            S.stage("scalar", _em(s_r2, [krow], [krow]))

            def v_r3(e):
                inv = rows[0:1, 0:Cw]
                t = rows[0:1, Cw:2 * Cw]
                q2 = vp
                e.reciprocal(out=q2, in_=q)
                e.tensor_tensor(out=t, in0=q2, in1=q2, op=ALU.mult)
                e.tensor_tensor(out=t, in0=t, in1=f, op=ALU.mult)
                e.tensor_scalar(out=t, in0=t, scalar1=-0.5, scalar2=1.5,
                                op0=ALU.mult, op1=ALU.add)
                e.tensor_tensor(out=q2, in0=q2, in1=t, op=ALU.mult)
                return e.tensor_tensor(out=inv, in0=inv, in1=q2, op=ALU.mult)
            S.stage("vector", _em(v_r3, [krow], [krow]))

        def v_r4(e):
            inv = rows[0:1, 0:Cw]
            b = rows[0:1, Cw:2 * Cw]
            e.tensor_tensor(out=b, in0=mu, in1=inv, op=ALU.mult)
            return e.tensor_scalar_mul(out=b, in0=b, scalar1=-1.0)
        S.stage("vector", _em(v_r4, [krow], [krow]))

        pbr = bufs["pbr"]

        def t_br(e):
            e.matmul(pbr[0], gonesr, rows[0:1, 0:Cw], start=True, stop=True)
            return e.matmul(pbr[1], gonesr, rows[0:1, Cw:2 * Cw],
                            start=True, stop=True)
        S.stage("tensor", _em(t_br, [krow, "gones", krep], [krep]))

    def emit_norm(name, x_ap, rep, xh_ap, ntile, tag, xh_key=None, xkey=None):
        kx, krep = xkey or (name + "_x"), tag + "_rep"
        kxh = xh_key or (name + "_xh")
        Cw = x_ap.shape[2]

        def v_n(e):
            s3 = rep[0].rearrange("p (a c) -> p a c", c=Cw) \
                .broadcast_to([128, ntile, Cw])
            b3 = rep[1].rearrange("p (a c) -> p a c", c=Cw) \
                .broadcast_to([128, ntile, Cw])
            e.tensor_tensor(out=xh_ap, in0=x_ap, in1=s3, op=ALU.mult)
            return e.tensor_tensor(out=xh_ap, in0=xh_ap, in1=b3, op=ALU.add)
        S.stage("vector", _em(v_n, [kx, krep], [kxh]))

    def dma_stage(eng, dmas, reads, writes):
        S.dma(eng, dmas, reads, writes)

    dma_stage("sync", [
        (c_l0lnw, l0_lnw), (c_l0lnb, l0_lnb), (c_bin, l0_bin),
        (c_convb, l0_convb), (c_d0, l0_d), (c_bout, l0_bout),
        (c_b1r, f0_b1), (c_b2r, f0_b2),
        (c_ln1g, l1_ln1g), (c_ln1b, l1_ln1b), (c_lng, l1_lng),
        (c_lnb, l1_lnb), (c_w1b, l1_w1b), (c_v1b, l1_v1b),
        (c_w2b, l1_w2b), (c_dtb, l1_dtb), (c_dp, l1_dp),
        (c_ln2g, l1_ln2g), (c_ln2b, l1_ln2b), (c_ff1b, l1_ff1b),
        (c_ff2b, l1_ff2b),
        (ga0, l0_alog.rearrange("p a n -> p (a n)")),
        (gA1, l1_alog.rearrange("p a n -> p (a n)")),
        (gcw0, l0_convw.rearrange("p a k -> p (a k)")),
        (gcw1, l1_convw.rearrange("p a k -> p (a k)")),
        (gbx[:, 0:1], l0_bx),
        (gbx[0:16, 1:2], l1_bpb), (gbx[0:16, 2:3], l1_cpb),
    ], [], ["gv", "ga0", "gA1", "gcw0", "gcw1", "gbx"])

    def p0_v(e):
        e.memset(gst0, 0.0)
        e.memset(gst1, 0.0)
        for i, v in enumerate([-1.0, -LN2C, 2.0 * LN2C, LN2C, -2.0 * LN2C,
                               HALFPI, 1.0]):
            e.memset(gk[:, i:i + 1], v)
        e.memset(gonesr, 1.0)
        return e.memset(gones, 1.0)
    S.stage("vector", _em(p0_v, [], ["gst0", "gst1", "gones", "gk"]))

    def p0_s(e):
        e.activation(out=ga0, in_=ga0, func=AF.Exp)
        e.activation(out=ga0, in_=ga0, func=AF.Exp, scale=K_NEG1)
        return e.activation(out=gA1, in_=gA1, func=AF.Exp)
    S.stage("scalar", _em(p0_s, ["ga0", "gA1", "gk"], ["ga0", "gA1"]))

    def p0_v2(e):
        return e.tensor_scalar_mul(out=gA1, in0=gA1, scalar1=-1.0)
    S.stage("vector", _em(p0_v2, ["gA1"], ["gA1"]))

    emit_fq("gv", c_b1r, c_b1q, gfqt, True, "gv", "gfqt")
    emit_fq("gv", c_b2r, c_b2q, gfqt, True, "gv", "gfqt")

    with ExitStack() as _stk:
        h_win = _stk.enter_context(nc.sbuf_tensor("a_win", [128, 8, 2 * DI], BF16))
        h_wx = _stk.enter_context(nc.sbuf_tensor("a_wx", [128, 16, 32], F32))
        h_x = _stk.enter_context(nc.sbuf_tensor("a_x", [128, 8, C], F32))
        h_sq = _stk.enter_context(nc.sbuf_tensor("a_sq", [128, 8, C], F32))
        h_xn = _stk.enter_context(nc.sbuf_tensor("a_xn", [128, 8, C], BF16))
        h_rows = _stk.enter_context(nc.sbuf_tensor("a_rows", [1, 2 * C], F32))
        h_mu = _stk.enter_context(nc.sbuf_tensor("a_mu", [1, C], F32))
        h_vp = _stk.enter_context(nc.sbuf_tensor("a_vp", [1, C], F32))
        h_q = _stk.enter_context(nc.sbuf_tensor("a_q", [1, C], F32))
        h_rep = _stk.enter_context(nc.sbuf_tensor("a_rep", [128, 2 * C], F32))
        h_xm = _stk.enter_context(nc.sbuf_tensor("a_xm", [128, 16, C + 3], F32))
        h_cv = _stk.enter_context(nc.sbuf_tensor("a_cv", [128, 16, C], F32))
        h_g16 = _stk.enter_context(nc.sbuf_tensor("a_g16", [128, 16, C], BF16))
        h_bc = _stk.enter_context(nc.sbuf_tensor("a_bc", [32, C], BF16))
        h_sta = _stk.enter_context(nc.psum_tensor("a_sta", [1, C], F32))
        h_apbr0 = _stk.enter_context(nc.psum_tensor("a_pbr0", [128, C], F32))
        h_apbr1 = _stk.enter_context(nc.psum_tensor("a_pbr1", [128, C], F32))
        h_stb = _stk.enter_context(nc.psum_tensor("a_stb", [1, C], F32))
        h_ps0 = _stk.enter_context(nc.psum_tensor("a_ps0", [128, C], F32))
        h_ps1 = _stk.enter_context(nc.psum_tensor("a_ps1", [128, C], F32))
        h_bps = _stk.enter_context(nc.psum_tensor("a_bps", [32, C], F32))
        blk = _stk.enter_context(nc.Block())
        a_win = h_win.ap(); a_wx = h_wx.ap(); a_x = h_x.ap(); a_sq = h_sq.ap()
        a_xh = a_x; a_rb = a_sq; a_xn = h_xn.ap()
        a_rows = h_rows.ap(); a_mu = h_mu.ap(); a_vp = h_vp.ap()
        a_q = h_q.ap(); a_f = h_stb.ap(); a_rep = h_rep.ap(); a_xm = h_xm.ap()
        a_cv = h_cv.ap(); a_g16 = h_g16.ap()
        a_bc = h_bc.ap()
        a_pss = [h_ps0.ap(), h_ps1.ap()]
        a_bps = h_bps.ap()
        lnb = {"pbr": [h_apbr0.ap(), h_apbr1.ap()], "tag": "lnA", "sq": a_sq, "sta": h_sta.ap(), "stb": h_stb.ap(), "mu": a_mu,
               "vp": a_vp, "q": a_q, "f": a_f, "rows": a_rows, "rep": a_rep}

        dma_stage("gpsimd", [
            (a_win, l0_win.rearrange("(a b) m -> b a m", b=128)),
            (a_wx, l0_wx.rearrange("(a b) m -> b a m", b=128)),
        ], [], ["a_win", "a_wx"])

        def v_xm0(e):
            return e.memset(a_xm[:, :, 0:3], 0.0)
        S.stage("vector", _em(v_xm0, [], ["A_xm"]))

        for j in range(nch):
            sl = slice(j * C, (j + 1) * C)
            dma_stage("sync", [(a_x, xin[:, :, sl])], [], ["A_x"])
            emit_ln("A", a_x, 8, lnb, second=True)

            def v_rb(e):
                last = None
                for dt in range(8):
                    e.tensor_tensor(out=a_rb[:, dt, :], in0=a_x[:, dt, :],
                                    in1=h_apbr0.ap(), op=ALU.mult)
                    e.tensor_tensor(out=a_rb[:, dt, :], in0=a_rb[:, dt, :],
                                    in1=h_apbr1.ap(), op=ALU.add)
                    last = e.tensor_tensor(out=a_rb[:, dt, :],
                                           in0=a_rb[:, dt, :],
                                           in1=a_x[:, dt, :], op=ALU.add)
                return last
            S.stage("vector", _em(v_rb, ["A_x", "lnA_rep"], ["lnA_sq"]))
            dma_stage("scalar", [(sc_r0[:, :, sl], a_rb)], ["lnA_sq"], ["sc_r0"])
            emit_norm("A", a_x, lnb["pbr"], a_xh, 8, "lnA", xh_key="A_x")

            def s_xn(e):
                last = None
                for dt in range(8):
                    last = e.activation(out=a_xn[:, dt, :], in_=a_xh[:, dt, :],
                                        func=AF.Identity,
                                        scale=c_l0lnw[:, dt:dt + 1],
                                        bias=c_l0lnb[:, dt:dt + 1])
                return last
            S.stage("scalar", _em(s_xn, ["A_x", "gv"], ["A_xn"]))

            for m in range(32):
                ps = a_pss[m % 2]
                kps = "A_ps%d" % (m % 2)

                def t_mm(e, m=m, ps=ps):
                    last = None
                    for k in range(8):
                        last = e.matmul(ps, a_win[:, k, m * 128:(m + 1) * 128],
                                        a_xn[:, k, :], start=(k == 0),
                                        stop=(k == 7))
                    return last
                S.stage("tensor", _em(t_mm, ["A_xn", "a_win"], [kps]))

                def s_ev(e, m=m, ps=ps):
                    if m < 16:
                        return e.activation(out=a_xm[:, m, 3:], in_=ps,
                                            func=AF.Identity, scale=1.0,
                                            bias=c_bin[:, m:m + 1])
                    return e.activation(out=a_g16[:, m - 16, :], in_=ps,
                                        func=AF.Silu, scale=1.0,
                                        bias=c_bin[:, m:m + 1])
                wkey = "A_xm" if m < 16 else "A_g16"
                S.stage("scalar", _em(s_ev, [kps, "gv"], [wkey]))

            def v_cv(e):
                for it in range(16):
                    e.tensor_scalar(out=a_cv[:, it, :], in0=a_xm[:, it, 0:C],
                                    scalar1=gcw0[:, 4 * it:4 * it + 1],
                                    scalar2=None, op0=ALU.mult)
                    for k in range(1, 4):
                        e.scalar_tensor_tensor(
                            out=a_cv[:, it, :], in0=a_xm[:, it, k:k + C],
                            scalar=gcw0[:, 4 * it + k:4 * it + k + 1],
                            in1=a_cv[:, it, :], op0=ALU.mult, op1=ALU.add)
                return e.tensor_copy(out=a_xm[:, :, 0:3],
                                     in_=a_xm[:, :, C:C + 3])
            S.stage("vector", _em(v_cv, ["A_xm", "gcw0"], ["A_cv", "A_xm"]))

            def s_cv(e):
                last = None
                for it in range(16):
                    last = e.activation(out=a_cv[:, it, :], in_=a_cv[:, it, :],
                                        func=AF.Silu, scale=1.0,
                                        bias=c_convb[:, it:it + 1])
                return last
            S.stage("scalar", _em(s_cv, ["A_cv", "gv"], ["A_cv"]))

            def t_wx(e):
                last = None
                for k in range(16):
                    last = e.matmul(a_bps, a_wx[:, k, :], a_cv[:, k, :],
                                    start=(k == 0), stop=(k == 15))
                return last
            S.stage("tensor", _em(t_wx, ["A_cv", "a_wx"], ["A_bps"]))

            def s_bc(e):
                return e.activation(out=a_bc, in_=a_bps, func=AF.Identity,
                                    scale=1.0, bias=gbx[:, 0:1])
            S.stage("scalar", _em(s_bc, ["A_bps", "gbx"], ["A_bc"]))

            dma_stage("gpsimd", [(sc_xconv[:, :, sl], a_cv)],
                      ["A_cv"], ["sc_xconv"])
            dma_stage("scalar", [
                (sc_gate[:, :, sl], a_g16),
                (sc_b0[:, sl], a_bc),
            ], ["A_g16", "A_bc"], ["sc_gate", "sc_b0"])

        S.run(blk)

    with ExitStack() as _stk:
        h_bxc = _stk.enter_context(nc.sbuf_tensor("b_xc16", [128, 2, 16, CS], BF16))
        h_bb = _stk.enter_context(nc.sbuf_tensor("b_brep", [128, 2, 16, CS], BF16))
        h_bcr = _stk.enter_context(nc.sbuf_tensor("b_crep", [128, 2, 16, CS], BF16))
        h_bu = _stk.enter_context(nc.sbuf_tensor("b_u", [128, CS], BF16))
        NDVE = 8
        h_bh = _stk.enter_context(nc.sbuf_tensor("b_h", [128, 2, NDVE, CS], F32))
        h_bhv = _stk.enter_context(nc.sbuf_tensor("b_hv", [128, 16 - NDVE, CS], F32))
        h_bua = _stk.enter_context(nc.sbuf_tensor("b_ua", [128, 16, CS], BF16))
        h_bz = _stk.enter_context(nc.sbuf_tensor("b_z", [128, 16 - NDVE, CS], F32))
        h_bzp = _stk.enter_context(nc.sbuf_tensor("b_zp", [128, CS], F32))
        h_byp = _stk.enter_context(nc.sbuf_tensor("b_yp", [128, CS], F32))
        h_by0 = _stk.enter_context(nc.sbuf_tensor("b_y0", [128, CS], F32))
        h_by1 = _stk.enter_context(nc.sbuf_tensor("b_y1", [128, CS], F32))
        blk = _stk.enter_context(nc.Block())
        b_xc16 = h_bxc.ap(); b_brep = h_bb.ap(); b_crep = h_bcr.ap()
        b_u = h_bu.ap(); b_h = h_bh.ap(); b_hv = h_bhv.ap()
        b_ua = h_bua.ap(); b_z = h_bz.ap()
        b_zp = h_bzp.ap(); b_yp = h_byp.ap()
        b_ys = [h_by0.ap(), h_by1.ap()]

        for j in range(nsc):
            sl = slice(j * CS, (j + 1) * CS)
            jp = j % 2
            xc_j = b_xc16[:, jp, :, :]
            br_j = b_brep[:, jp, :, :]
            cr_j = b_crep[:, jp, :, :]
            kxc = "B_xc%d" % jp
            krep = "B_rep%d" % jp
            dma_stage("sync", [(xc_j, sc_xconv[:, :, sl])],
                      ["sc_xconv"], [kxc])
            dma_stage("sync",
                      [(br_j[:, n, :],
                        sc_b0[n:n + 1, sl].partition_broadcast(128))
                       for n in range(16)] +
                      [(cr_j[:, n, :],
                        sc_b0[16 + n:17 + n, sl].partition_broadcast(128))
                       for n in range(16)],
                      ["sc_b0"], [krep])

            for it in range(16):
                p = it % 2
                yb = b_ys[p]
                ky = "B_y%d" % p
                kh = "B_h%d" % p
                hcur = b_h[:, p, :, :]

                def v_scan(e, it=it, yb=yb, hcur=hcur, xc_j=xc_j,
                           br_j=br_j, cr_j=cr_j):
                    e.tensor_tensor(
                        out=b_ua,
                        in0=xc_j[:, it:it + 1, :].broadcast_to([128, 16, CS]),
                        in1=br_j, op=ALU.mult)
                    for n in range(16):
                        cidx = 16 * it + n
                        hdst = b_hv[:, n - NDVE, :] if n >= NDVE \
                            else hcur[:, n, :]
                        e.tensor_tensor_scan(
                            out=hdst,
                            data0=ga0[:, cidx:cidx + 1].broadcast_to([128, CS]),
                            data1=b_ua[:, n, :], initial=gst0[:, cidx:cidx + 1],
                            op0=ALU.mult, op1=ALU.add)
                        if n == NDVE - 1:
                            e.tensor_copy(
                                out=gst0[:, 16 * it:16 * it + NDVE]
                                .rearrange("p (a c) -> p a c", c=1),
                                in_=hcur[:, :, CS - 1:CS])
                    e.tensor_copy(
                        out=gst0[:, 16 * it + NDVE:16 * it + 16]
                        .rearrange("p (a c) -> p a c", c=1),
                        in_=b_hv[:, 0:16 - NDVE, CS - 1:CS])
                    zf = b_z.rearrange("p a c -> p (a c)")
                    e.tensor_tensor(out=b_z, in0=b_hv,
                                    in1=cr_j[:, NDVE:16, :], op=ALU.mult)
                    e.tensor_tensor(out=zf[:, 0:4 * CS], in0=zf[:, 0:4 * CS],
                                    in1=zf[:, 4 * CS:8 * CS], op=ALU.add)
                    e.tensor_tensor(out=zf[:, 0:2 * CS], in0=zf[:, 0:2 * CS],
                                    in1=zf[:, 2 * CS:4 * CS], op=ALU.add)
                    return e.tensor_tensor(out=yb, in0=zf[:, 0:CS],
                                           in1=zf[:, CS:2 * CS], op=ALU.add)
                S.stage("vector", _em(v_scan,
                                      [kxc, krep, "ga0", "gst0", ky, kh],
                                      [ky, kh, "gst0"]))

                def g_con(e, it=it, yb=yb, hcur=hcur, cr_j=cr_j):
                    e.tensor_tensor(out=hcur, in0=hcur,
                                    in1=cr_j[:, 0:NDVE, :], op=ALU.mult)
                    hf = hcur.rearrange("p a c -> p (a c)")
                    e.tensor_tensor(out=hf[:, 0:4 * CS], in0=hf[:, 0:4 * CS],
                                    in1=hf[:, 4 * CS:8 * CS], op=ALU.add)
                    e.tensor_tensor(out=hf[:, 0:2 * CS], in0=hf[:, 0:2 * CS],
                                    in1=hf[:, 2 * CS:4 * CS], op=ALU.add)
                    e.tensor_tensor(out=hf[:, 0:CS], in0=hf[:, 0:CS],
                                    in1=hf[:, CS:2 * CS], op=ALU.add)
                    return e.tensor_tensor(out=yb, in0=yb, in1=hf[:, 0:CS],
                                           op=ALU.add)
                S.stage("gpsimd", _em(g_con, [kh, ky, krep], [ky, kh]))
                dma_stage("scalar", [(sc_y0[:, it, sl], yb)], [ky], ["sc_y0"])

        S.run(blk)

    with ExitStack() as _stk:
        C2 = C // 2
        h_cwo = _stk.enter_context(nc.sbuf_tensor("c_wo", [128, 16, D], BF16))
        h_cy = _stk.enter_context(nc.sbuf_tensor("c_y", [128, 2, 16, C2], F32))
        h_cxc = _stk.enter_context(nc.sbuf_tensor("c_xc", [128, 2, 16, C2], BF16))
        h_cg = _stk.enter_context(nc.sbuf_tensor("c_g16", [128, 2, 16, C2], BF16))
        h_cyg = _stk.enter_context(nc.sbuf_tensor("c_yg", [128, 16, C2], BF16))
        h_crb = _stk.enter_context(nc.sbuf_tensor("c_rb", [128, 2, 8, C2], F32))
        h_cs1 = _stk.enter_context(nc.sbuf_tensor("c_s1", [128, 2, 8, C2], F32))
        h_cps0 = _stk.enter_context(nc.psum_tensor("c_ps0", [128, C2], F32))
        h_cps1 = _stk.enter_context(nc.psum_tensor("c_ps1", [128, C2], F32))
        blk = _stk.enter_context(nc.Block())
        c_wo = h_cwo.ap(); c_y = h_cy.ap(); c_xc = h_cxc.ap()
        c_g16 = h_cg.ap(); c_yg = h_cyg.ap(); c_rb = h_crb.ap()
        c_s1 = h_cs1.ap(); c_pss = [h_cps0.ap(), h_cps1.ap()]

        dma_stage("gpsimd",
                  [(c_wo, l0_wout.rearrange("(a b) m -> b a m", b=128))],
                  [], ["c_wo"])

        for j in range(2 * nch):
            sl = slice(j * C2, (j + 1) * C2)
            jp = j % 2
            cy_j = c_y[:, jp, :, :]
            cxc_j = c_xc[:, jp, :, :]
            cg_j = c_g16[:, jp, :, :]
            crb_j = c_rb[:, jp, :, :]
            cs1_j = c_s1[:, jp, :, :]
            kld = "C_ld%d" % jp
            ks1 = "C_s1%d" % jp
            dma_stage("sync", [
                (cy_j, sc_y0[:, :, sl]),
                (cxc_j, sc_xconv[:, :, sl]),
                (cg_j, sc_gate[:, :, sl]),
                (crb_j, sc_r0[:, :, sl]),
            ], ["sc_y0", "sc_xconv", "sc_gate", "sc_r0"], [kld])

            def v_g(e, cy_j=cy_j, cxc_j=cxc_j, cg_j=cg_j):
                for it in range(16):
                    e.scalar_tensor_tensor(out=cy_j[:, it, :],
                                           in0=cxc_j[:, it, :],
                                           scalar=c_d0[:, it:it + 1],
                                           in1=cy_j[:, it, :],
                                           op0=ALU.mult, op1=ALU.add)
                return e.tensor_tensor(out=c_yg, in0=cy_j, in1=cg_j,
                                       op=ALU.mult)
            S.stage("vector", _em(v_g, [kld, "gv"], ["C_yg", kld]))

            for m in range(8):
                ps = c_pss[m % 2]
                kps = "C_ps%d" % (m % 2)

                def t_mm(e, m=m, ps=ps):
                    last = None
                    for k in range(16):
                        last = e.matmul(ps, c_wo[:, k, m * 128:(m + 1) * 128],
                                        c_yg[:, k, :], start=(k == 0),
                                        stop=(k == 15))
                    return last
                S.stage("tensor", _em(t_mm, ["C_yg", "c_wo"], [kps]))

                def v_ev(e, m=m, ps=ps, crb_j=crb_j, cs1_j=cs1_j):
                    return e.scalar_tensor_tensor(out=cs1_j[:, m, :], in0=ps,
                                                  scalar=c_bout[:, m:m + 1],
                                                  in1=crb_j[:, m, :],
                                                  op0=ALU.add, op1=ALU.add)
                S.stage("vector", _em(v_ev, [kps, kld, "gv"], [ks1]))

            dma_stage("scalar", [(sc_s1[:, :, sl], cs1_j)], [ks1], ["sc_s1"])

        S.run(blk)

    with ExitStack() as _stk:
        h_fw1 = _stk.enter_context(nc.sbuf_tensor("f_w1", [128, 8, 4 * D], BF16))
        h_fwt = _stk.enter_context(nc.sbuf_tensor("f_wt", [128, 8, 256], F32))
        h_ffqw = _stk.enter_context(nc.sbuf_tensor("f_fqw", [128, 2 * 2048], F32))
        h_fx = _stk.enter_context(nc.sbuf_tensor("f_x", [128, 2, 8, CF], F32))
        h_fsq = _stk.enter_context(nc.sbuf_tensor("f_sq", [128, 8, CF], F32))
        h_fxq = _stk.enter_context(nc.sbuf_tensor("f_xq", [128, 8, CF], BF16))
        h_frows = _stk.enter_context(nc.sbuf_tensor("f_rows", [1, 2 * CF], F32))
        h_fmu = _stk.enter_context(nc.sbuf_tensor("f_mu", [1, CF], F32))
        h_fvp = _stk.enter_context(nc.sbuf_tensor("f_vp", [1, CF], F32))
        h_fq = _stk.enter_context(nc.sbuf_tensor("f_q", [1, CF], F32))
        h_frep = _stk.enter_context(nc.sbuf_tensor("f_rep", [128, 2 * CF], F32))
        h_fh16 = _stk.enter_context(nc.sbuf_tensor("f_h16", [128, 32, CF], BF16))
        h_fsta = _stk.enter_context(nc.psum_tensor("f_sta", [1, CF], F32))
        h_fpbr0 = _stk.enter_context(nc.psum_tensor("f_pbr0", [128, CF], F32))
        h_fpbr1 = _stk.enter_context(nc.psum_tensor("f_pbr1", [128, CF], F32))
        h_fstb = _stk.enter_context(nc.psum_tensor("f_stb", [1, CF], F32))
        h_fps0 = _stk.enter_context(nc.psum_tensor("f_ps0", [128, CF], F32))
        h_fps1 = _stk.enter_context(nc.psum_tensor("f_ps1", [128, CF], F32))
        blk = _stk.enter_context(nc.Block())
        f_w1 = h_fw1.ap(); f_wt = h_fwt.ap(); f_fqw = h_ffqw.ap()
        f_x = h_fx.ap(); f_sq = h_fsq.ap()
        f_xq = h_fxq.ap(); f_rows = h_frows.ap()
        f_mu = h_fmu.ap(); f_vp = h_fvp.ap(); f_q = h_fq.ap(); f_f = h_fstb.ap()
        f_rep = h_frep.ap(); f_h16 = h_fh16.ap()
        f_pss = [h_fps0.ap(), h_fps1.ap()]
        lnbf = {"pbr": [h_fpbr0.ap(), h_fpbr1.ap()], "tag": "lnF", "sq": f_sq, "sta": h_fsta.ap(), "stb": h_fstb.ap(),
                "mu": f_mu, "vp": f_vp, "q": f_q, "f": f_f,
                "rows": f_rows, "rep": f_rep}

        for si in range(16):
            csl = slice(si * 256, (si + 1) * 256)
            dma_stage("sync",
                      [(f_wt, f0_w1t.rearrange("(a b) m -> b a m", b=128)[:, :, csl])],
                      [], ["F_wt"])
            emit_fq("F_wt", f_wt, f_w1[:, :, csl], f_fqw, True, "f_w1",
                    "F_fqw")

        for j in range(ncf):
            sl = slice(j * CF, (j + 1) * CF)
            jp = j % 2
            x_j = f_x[:, jp, :, :]
            kx = "F_x%d" % jp
            dma_stage("sync", [(x_j, sc_s1[:, :, sl])], ["sc_s1"], [kx])
            emit_ln("F", x_j, 8, lnbf, second=False, xkey=kx)
            emit_norm("F", x_j, lnbf["pbr"], x_j, 8, "lnF", xh_key=kx, xkey=kx)
            emit_fq(kx, x_j[:, 0:4, :], f_xq[:, 0:4, :], f_fqw, True,
                    "F_xq", "F_fqw")
            emit_fq(kx, x_j[:, 4:8, :], f_xq[:, 4:8, :], f_fqw, True,
                    "F_xq", "F_fqw")

            for m in range(32):
                ps = f_pss[m % 2]
                kps = "F_ps%d" % (m % 2)

                def t_mm(e, m=m, ps=ps):
                    last = None
                    for k in range(8):
                        last = e.matmul(ps, f_w1[:, k, m * 128:(m + 1) * 128],
                                        f_xq[:, k, :], start=(k == 0),
                                        stop=(k == 7))
                    return last
                S.stage("tensor", _em(t_mm, ["F_xq", "f_w1"], [kps]))

                def s_ev(e, m=m, ps=ps):
                    return e.activation(out=f_h16[:, m, :], in_=ps,
                                        func=AF.Identity, scale=1.0,
                                        bias=c_b1q[:, m:m + 1])
                S.stage("scalar", _em(s_ev, [kps, "gv"], ["F_h16"]))

            for g in range(8):
                gsl = slice(g * 4, (g + 1) * 4)
                emit_fq("F_h16", f_h16[:, gsl, :], f_h16[:, gsl, :],
                        f_fqw, False, "F_h16", "F_fqw", ties_even=True)

            dma_stage("scalar", [(sc_hq[:, :, sl], f_h16)], ["F_h16"], ["sc_hq"])

        S.run(blk)

    with ExitStack() as _stk:
        h_gw2 = _stk.enter_context(nc.sbuf_tensor("g_w2", [128, 32, D], BF16))
        h_gwt = _stk.enter_context(nc.sbuf_tensor("g_wt", [128, 32, 64], F32))
        h_ghq = _stk.enter_context(nc.sbuf_tensor("g_hq", [128, 2, 32, C], BF16))
        h_gs1 = _stk.enter_context(nc.sbuf_tensor("g_s1", [128, 2, 8, C], F32))
        h_gmk = _stk.enter_context(nc.sbuf_tensor("g_mk", [128, 2, 8, C], BF16))
        h_gy2 = _stk.enter_context(nc.sbuf_tensor("g_y2", [128, 8, C], BF16))
        h_gps0 = _stk.enter_context(nc.psum_tensor("g_ps0", [128, C], F32))
        h_gps1 = _stk.enter_context(nc.psum_tensor("g_ps1", [128, C], F32))
        blk = _stk.enter_context(nc.Block())
        g_w2 = h_gw2.ap(); g_wt = h_gwt.ap()
        g_fqw = h_gs1.ap().rearrange("p a b c -> p (a b c)")[:, 0:4096]
        g_hq = h_ghq.ap(); g_s1 = h_gs1.ap(); g_mk = h_gmk.ap()
        g_y2 = h_gy2.ap()
        g_pss = [h_gps0.ap(), h_gps1.ap()]

        for si in range(16):
            csl = slice(si * 64, (si + 1) * 64)
            dma_stage("sync",
                      [(g_wt, f0_w2t.rearrange("(a b) m -> b a m", b=128)[:, :, csl])],
                      [], ["G_wt"])
            emit_fq("G_wt", g_wt, g_w2[:, :, csl], g_fqw, True, "g_w2",
                    "G_fqw")

        for j in range(nch):
            sl = slice(j * C, (j + 1) * C)
            jp = j % 2
            hq_j = g_hq[:, jp, :, :]
            s1_j = g_s1[:, jp, :, :]
            mk_j = g_mk[:, jp, :, :]
            khq = "G_hq%d" % jp
            ks1 = "G_s1%d" % jp
            dma_stage("sync", [
                (hq_j, sc_hq[:, :, sl]),
                (s1_j, sc_s1[:, :, sl]),
                (mk_j, mask16[:, :, sl]),
            ], ["sc_hq", "sc_s1"] + (["G_fqw", "g_w2"] if j == 0 else []),
                [khq, ks1])

            for m in range(8):
                ps = g_pss[m % 2]
                kps = "G_ps%d" % (m % 2)

                def t_mm(e, m=m, ps=ps, hq_j=hq_j):
                    last = None
                    for k in range(32):
                        last = e.matmul(ps, g_w2[:, k, m * 128:(m + 1) * 128],
                                        hq_j[:, k, :], start=(k == 0),
                                        stop=(k == 31))
                    return last
                S.stage("tensor", _em(t_mm, [khq, "g_w2"], [kps]))

                def s_ev(e, m=m, ps=ps):
                    return e.activation(out=g_y2[:, m, :], in_=ps,
                                        func=AF.Identity, scale=1.0,
                                        bias=c_b2q[:, m:m + 1])
                S.stage("scalar", _em(s_ev, [kps, "gv"], ["G_y2"]))

            def v_fin(e, s1_j=s1_j, mk_j=mk_j):
                e.tensor_tensor(out=g_y2, in0=g_y2, in1=mk_j, op=ALU.mult)
                return e.tensor_tensor(out=s1_j, in0=s1_j, in1=g_y2,
                                       op=ALU.add)
            S.stage("vector", _em(v_fin, ["G_y2", khq, ks1],
                                  ["G_y2", ks1]))

            dma_stage("scalar", [(sc_s2[:, :, sl], s1_j)], [ks1], ["sc_s2"])

        S.run(blk)

    with ExitStack() as _stk:
        h_lw1 = _stk.enter_context(nc.sbuf_tensor("l_w1", [128, 8, D], BF16))
        h_lv1 = _stk.enter_context(nc.sbuf_tensor("l_v1", [128, 8, D], BF16))
        h_lx = _stk.enter_context(nc.sbuf_tensor("l_x", [128, 2, 8, C], F32))
        h_lsq = _stk.enter_context(nc.sbuf_tensor("l_sq", [128, 8, C], F32))
        h_lxh = _stk.enter_context(nc.sbuf_tensor("l_xh", [128, 8, C], F32))
        h_lxn = _stk.enter_context(nc.sbuf_tensor("l_xn", [128, 8, C], F32))
        h_lxm = _stk.enter_context(nc.sbuf_tensor("l_xm", [128, 8, C], F32))
        h_lxm16 = _stk.enter_context(nc.sbuf_tensor("l_xm16", [128, 8, C], BF16))
        h_lrows = _stk.enter_context(nc.sbuf_tensor("l_rows", [1, 2 * C], F32))
        h_lmu = _stk.enter_context(nc.sbuf_tensor("l_mu", [1, C], F32))
        h_lvp = _stk.enter_context(nc.sbuf_tensor("l_vp", [1, C], F32))
        h_lq = _stk.enter_context(nc.sbuf_tensor("l_q", [1, C], F32))
        h_lrep = _stk.enter_context(nc.sbuf_tensor("l_rep", [128, 2 * C], F32))
        h_lx1 = _stk.enter_context(nc.sbuf_tensor("l_x1", [128, 8, C], F32))
        h_lv16 = _stk.enter_context(nc.sbuf_tensor("l_v16", [128, 8, C], BF16))
        h_lsta = _stk.enter_context(nc.psum_tensor("l_sta", [1, C], F32))
        h_lpbr0 = _stk.enter_context(nc.psum_tensor("l_pbr0", [128, C], F32))
        h_lpbr1 = _stk.enter_context(nc.psum_tensor("l_pbr1", [128, C], F32))
        h_lstb = _stk.enter_context(nc.psum_tensor("l_stb", [1, C], F32))
        h_lps0 = _stk.enter_context(nc.psum_tensor("l_ps0", [128, C], F32))
        h_lps1 = _stk.enter_context(nc.psum_tensor("l_ps1", [128, C], F32))
        blk = _stk.enter_context(nc.Block())
        l_w1 = h_lw1.ap(); l_v1 = h_lv1.ap()
        l_x = h_lx.ap(); l_sq = h_lsq.ap(); l_xh = h_lxh.ap()
        l_xn = h_lxn.ap(); l_xm = h_lxm.ap(); l_xm16 = h_lxm16.ap()
        l_rows = h_lrows.ap(); l_mu = h_lmu.ap(); l_vp = h_lvp.ap()
        l_q = h_lq.ap(); l_f = h_lstb.ap(); l_rep = h_lrep.ap()
        l_x1 = h_lx1.ap(); l_v16 = h_lv16.ap()
        l_pss = [h_lps0.ap(), h_lps1.ap()]
        lnb1 = {"pbr": [h_lpbr0.ap(), h_lpbr1.ap()], "tag": "lnL", "sq": l_sq, "sta": h_lsta.ap(), "stb": h_lstb.ap(),
                "mu": l_mu, "vp": l_vp, "q": l_q, "f": l_f,
                "rows": l_rows, "rep": l_rep}

        dma_stage("gpsimd", [
            (l_w1, l1_w1.rearrange("(a b) m -> b a m", b=128)),
            (l_v1, l1_v1.rearrange("(a b) m -> b a m", b=128)),
        ], [], ["l_w1", "l_v1"])

        for j in range(nch):
            sl = slice(j * C, (j + 1) * C)
            jp = j % 2
            lx_j = l_x[:, jp, :, :]
            klx = "L_x%d" % jp
            dma_stage("sync", [(lx_j, sc_s2[:, :, sl])], ["sc_s2"], [klx])
            emit_ln("L", lx_j, 8, lnb1, second=True, xkey=klx)
            emit_norm("L", lx_j, lnb1["pbr"], l_xh, 8, "lnL", xkey=klx)
            dma_stage("scalar", [(sc_xh1[:, :, sl], l_xh)], ["L_xh"], ["sc_xh1"])

            def s_xn(e):
                last = None
                for dt in range(8):
                    last = e.activation(out=l_xn[:, dt, :], in_=l_xh[:, dt, :],
                                        func=AF.Identity,
                                        scale=c_ln1g[:, dt:dt + 1],
                                        bias=c_ln1b[:, dt:dt + 1])
                return last
            S.stage("scalar", _em(s_xn, ["L_xh", "gv"], ["M_x"]))

            emit_ln("M", l_xn, 8, lnb1, second=False)
            emit_norm("M", l_xn, lnb1["pbr"], l_xm, 8, "lnL")

            def s_xm(e):
                last = None
                for dt in range(8):
                    last = e.activation(out=l_xm16[:, dt, :],
                                        in_=l_xm[:, dt, :],
                                        func=AF.Identity,
                                        scale=c_lng[:, dt:dt + 1],
                                        bias=c_lnb[:, dt:dt + 1])
                return last
            S.stage("scalar", _em(s_xm, ["M_xh", "gv"], ["M_xm"]))

            for wname, wap, bias_col, kind in (
                ("l_w1", l_w1, c_w1b, "x1"),
                ("l_v1", l_v1, c_v1b, "v"),
            ):
                for m in range(8):
                    ps = l_pss[m % 2]
                    kps = "L_ps%d" % (m % 2)

                    def t_mm(e, m=m, ps=ps, wap=wap):
                        last = None
                        for k in range(8):
                            last = e.matmul(ps, wap[:, k, m * 128:(m + 1) * 128],
                                            l_xm16[:, k, :], start=(k == 0),
                                            stop=(k == 7))
                        return last
                    S.stage("tensor", _em(t_mm, ["M_xm", wname], [kps]))

                    def s_ev(e, m=m, ps=ps, bias_col=bias_col, kind=kind):
                        if kind == "x1":
                            return e.activation(out=l_x1[:, m, :], in_=ps,
                                                func=AF.Identity, scale=1.0,
                                                bias=bias_col[:, m:m + 1])
                        return e.activation(out=l_v16[:, m, :], in_=ps,
                                            func=AF.Silu, scale=1.0,
                                            bias=bias_col[:, m:m + 1])
                    wk = {"x1": "L_x1", "v": "L_v16"}[kind]
                    S.stage("scalar", _em(s_ev, [kps, "gv"], [wk]))

            dma_stage("scalar", [(sc_x1[:, :, sl], l_x1)], ["L_x1"], ["sc_x1"])
            dma_stage("scalar", [(sc_v[:, :, sl], l_v16)], ["L_v16"], ["sc_v"])


        S.run(blk)

    with ExitStack() as _stk:
        h_mx1 = _stk.enter_context(nc.sbuf_tensor("m_x1h", [128, 8, CS + 2], F32))
        h_mdt = _stk.enter_context(nc.sbuf_tensor("m_dt", [128, 8, D], BF16))
        h_mbp = _stk.enter_context(nc.sbuf_tensor("m_bp", [128, 8, NS], BF16))
        h_mcp = _stk.enter_context(nc.sbuf_tensor("m_cp", [128, 8, NS], BF16))
        h_mxc16 = _stk.enter_context(nc.sbuf_tensor("m_xc16", [128, 8, CS], BF16))
        h_mbc = _stk.enter_context(nc.sbuf_tensor("m_bc16", [16, 2 * CS], BF16))
        h_meye = _stk.enter_context(nc.sbuf_tensor("m_eye", [16, 16 * 128], BF16))
        h_mpbc0 = _stk.enter_context(nc.psum_tensor("m_pbc0", [128, CS], F32))
        h_mpbc1 = _stk.enter_context(nc.psum_tensor("m_pbc1", [128, CS], F32))
        h_mps0 = _stk.enter_context(nc.psum_tensor("m_ps0", [128, CS], F32))
        h_mps1 = _stk.enter_context(nc.psum_tensor("m_ps1", [128, CS], F32))
        h_mpsb = _stk.enter_context(nc.psum_tensor("m_psb", [16, CS], F32))
        h_mpsc = _stk.enter_context(nc.psum_tensor("m_psc", [16, CS], F32))
        h_mdl = _stk.enter_context(nc.sbuf_tensor("m_dl", [128, 8, CS], F32))
        h_mcv = _stk.enter_context(nc.sbuf_tensor("m_cv", [128, 8, CS], F32))
        h_mw16 = _stk.enter_context(nc.sbuf_tensor("m_w16", [128, 8, CS], BF16))
        h_mb = _stk.enter_context(nc.sbuf_tensor("m_brep", [128, 16, CS], BF16))
        h_mc = _stk.enter_context(nc.sbuf_tensor("m_crep", [128, 16, CS], BF16))
        h_mda0 = _stk.enter_context(nc.sbuf_tensor("m_dA0", [128, CS], F32))
        h_mda1 = _stk.enter_context(nc.sbuf_tensor("m_dA1", [128, CS], F32))
        NPOOL1 = 9
        h_mh = _stk.enter_context(nc.sbuf_tensor("m_h", [128, 2, NPOOL1, CS], F32))
        h_mhv = _stk.enter_context(nc.sbuf_tensor("m_hv", [128, 16 - NPOOL1, CS], F32))
        h_mua = _stk.enter_context(nc.sbuf_tensor("m_ua", [128, 16, CS], BF16))
        h_mzp = _stk.enter_context(nc.sbuf_tensor("m_zp", [128, CS], F32))
        h_myp = _stk.enter_context(nc.sbuf_tensor("m_yp", [128, CS], F32))
        h_my0 = _stk.enter_context(nc.sbuf_tensor("m_y0", [128, CS], F32))
        h_my1 = _stk.enter_context(nc.sbuf_tensor("m_y1", [128, CS], F32))
        blk = _stk.enter_context(nc.Block())
        m_x1h = h_mx1.ap(); m_dl = h_mdl.ap(); m_cv = h_mcv.ap()
        m_dt = h_mdt.ap(); m_bp = h_mbp.ap(); m_cp = h_mcp.ap()
        m_xc16 = h_mxc16.ap(); m_sp = h_mh.ap()[:, 0, 0:8, :]; m_bc16 = h_mbc.ap()
        m_pss = [h_mps0.ap(), h_mps1.ap()]
        m_psb = h_mpsb.ap(); m_psc = h_mpsc.ap()
        m_w16 = h_mw16.ap(); m_brep = h_mb.ap(); m_crep = h_mc.ap()
        m_eye = h_meye.ap()
        m_pbcs = [h_mpbc0.ap(), h_mpbc1.ap()]
        m_dAs = [h_mda0.ap(), h_mda1.ap()]
        m_h = h_mh.ap(); m_hv = h_mhv.ap()
        m_ua = h_mua.ap()
        m_zp = h_mzp.ap(); m_yp = h_myp.ap()
        m_ys = [h_my0.ap(), h_my1.ap()]

        dma_stage("sync", [(m_eye, bceye)], [], ["m_eye"])
        dma_stage("gpsimd", [
            (m_dt, l1_dtw.rearrange("(a b) m -> b a m", b=128)),
            (m_bp, l1_bpw.rearrange("(a b) m -> b a m", b=128)),
            (m_cp, l1_cpw.rearrange("(a b) m -> b a m", b=128)),
        ], [], ["m_dt", "m_bp", "m_cp"])

        for j in range(nsc):
            sl = slice(j * CS, (j + 1) * CS)

            lo = max(0, j * CS - 1)
            hi = min(T, j * CS + CS + 1)
            d0 = 1 if j == 0 else 0
            d1 = CS + 1 if j == nsc - 1 else CS + 2
            if j == 0 or j == nsc - 1:
                def v_zh(e, j=j):
                    last = None
                    if j == 0:
                        last = e.memset(m_x1h[:, :, 0:1], 0.0)
                    if j == nsc - 1:
                        last = e.memset(m_x1h[:, :, CS + 1:CS + 2], 0.0)
                    return last
                S.stage("vector", _em(v_zh, [], ["M_x1h"]))
            dma_stage("sync", [(m_x1h[:, :, d0:d1], sc_x1[:, :, lo:hi])],
                      ["sc_x1"], ["M_x1h"])
            def v_cv(e):
                for dt in range(8):
                    e.tensor_scalar(out=m_cv[:, dt, :], in0=m_x1h[:, dt, 0:CS],
                                    scalar1=gcw1[:, 3 * dt:3 * dt + 1],
                                    scalar2=None, op0=ALU.mult)
                    for k in range(1, 3):
                        e.scalar_tensor_tensor(
                            out=m_cv[:, dt, :], in0=m_x1h[:, dt, k:k + CS],
                            scalar=gcw1[:, 3 * dt + k:3 * dt + k + 1],
                            in1=m_cv[:, dt, :], op0=ALU.mult, op1=ALU.add)
                return None
            S.stage("vector", _em(v_cv, ["M_x1h", "gcw1"], ["M_cv"]))

            def s_cv(e):
                return e.activation(out=m_cv.rearrange("p a c -> p (a c)"),
                                    in_=m_cv.rearrange("p a c -> p (a c)"),
                                    func=AF.Silu)
            S.stage("scalar", _em(s_cv, ["M_cv"], ["M_cv"]))

            def v_xc16(e):
                return e.tensor_copy(out=m_xc16.rearrange("p a c -> p (a c)"),
                                     in_=m_cv.rearrange("p a c -> p (a c)"))
            S.stage("vector", _em(v_xc16, ["M_cv"], ["M_xc16"]))

            for m in range(8):
                ps = m_pss[m % 2]
                kps = "M_ps%d" % (m % 2)

                def t_mm(e, m=m, ps=ps):
                    last = None
                    for k in range(8):
                        last = e.matmul(ps, m_dt[:, k, m * 128:(m + 1) * 128],
                                        m_xc16[:, k, :], start=(k == 0),
                                        stop=(k == 7))
                    return last
                S.stage("tensor", _em(t_mm, ["M_xc16", "m_dt"], [kps]))

                def s_ev(e, m=m, ps=ps):
                    return e.activation(out=m_dl[:, m, :], in_=ps,
                                        func=AF.Identity, scale=1.0,
                                        bias=c_dtb[:, m:m + 1])
                S.stage("scalar", _em(s_ev, [kps, "gv"], ["M_dl"]))

            def s_sp(e):
                fl = m_dl.rearrange("p a c -> p (a c)")
                t1 = m_sp.rearrange("p a c -> p (a c)")
                e.activation(out=t1, in_=fl, func=AF.Abs)
                e.activation(out=t1, in_=t1, func=AF.Exp, scale=K_NEG1)
                return e.activation(out=t1, in_=t1, func=AF.Ln, bias=K_ONE)
            S.stage("scalar", _em(s_sp, ["M_dl", "M_h0", "M_h1"],
                                  ["M_sp", "M_h0", "M_h1"]))

            def v_sp(e):
                fl = m_dl.rearrange("p a c -> p (a c)")
                t1 = m_sp.rearrange("p a c -> p (a c)")
                e.tensor_scalar_max(out=fl, in0=fl, scalar1=0.0)
                return e.tensor_tensor(out=fl, in0=fl, in1=t1, op=ALU.add)
            S.stage("vector", _em(v_sp, ["M_dl", "M_sp"], ["M_dl"]))

            def t_bp(e):
                for k in range(8):
                    e.matmul(m_psb, m_bp[:, k, :], m_xc16[:, k, :],
                             start=(k == 0), stop=(k == 7))
                last = None
                for k in range(8):
                    last = e.matmul(m_psc, m_cp[:, k, :], m_xc16[:, k, :],
                                    start=(k == 0), stop=(k == 7))
                return last
            S.stage("tensor", _em(t_bp, ["M_xc16", "m_bp", "m_cp"], ["M_psbc"]))

            def s_bc(e):
                e.activation(out=m_bc16[:, 0:CS], in_=m_psb, func=AF.Identity,
                             scale=1.0, bias=gbx[0:16, 1:2])
                return e.activation(out=m_bc16[:, CS:2 * CS], in_=m_psc,
                                    func=AF.Identity, scale=1.0,
                                    bias=gbx[0:16, 2:3])
            S.stage("scalar", _em(s_bc, ["M_psbc", "gbx"], ["M_bc"]))

            dma_stage("scalar", [
                (sc_bc1[:, j * CS:(j + 1) * CS], m_bc16[:, 0:CS]),
                (sc_bc1[:, T + j * CS:T + (j + 1) * CS], m_bc16[:, CS:2 * CS]),
            ], ["M_bc"], ["sc_bc1"])
            for half, rep in ((0, m_brep), (1, m_crep)):
                for n in range(16):
                    pb = m_pbcs[n % 2]
                    kpb = "M_pbc%d" % (n % 2)

                    def t_bc(e, n=n, half=half, pb=pb):
                        return e.matmul(
                            pb, m_eye[:, n * 128:(n + 1) * 128],
                            m_bc16[:, half * CS:half * CS + CS],
                            start=True, stop=True)
                    S.stage("tensor", _em(t_bc, ["M_bc", "m_eye", kpb],
                                          [kpb]))

                    def s_bc2(e, n=n, pb=pb, rep=rep):
                        return e.activation(out=rep[:, n, :], in_=pb,
                                            func=AF.Identity, scale=1.0)
                    S.stage("scalar", _em(s_bc2, [kpb], ["M_rep", kpb]))

            def v_w(e):
                return e.tensor_tensor(out=m_w16.rearrange("p a c -> p (a c)"),
                                       in0=m_dl.rearrange("p a c -> p (a c)"),
                                       in1=m_cv.rearrange("p a c -> p (a c)"),
                                       op=ALU.mult)
            S.stage("vector", _em(v_w, ["M_dl", "M_cv"], ["M_w16"]))

            for dt in range(8):
                p = dt % 2
                yb = m_ys[p]
                ky = "M_y%d" % p
                kh = "M_h%d" % p
                hcur = m_h[:, p, :, :]

                def v_u(e, dt=dt):
                    return e.tensor_tensor(
                        out=m_ua,
                        in0=m_w16[:, dt:dt + 1, :].broadcast_to([128, 16, CS]),
                        in1=m_brep, op=ALU.mult)
                S.stage("vector", _em(v_u, ["M_w16", "M_rep"], ["M_ua"]))

                for n in range(NS):
                    da = m_dAs[n % 2]
                    kda = "M_dA%d" % (n % 2)
                    cidx = NS * dt + n

                    def s_da(e, dt=dt, n=n, da=da, cidx=cidx):
                        return e.activation(out=da, in_=m_dl[:, dt, :],
                                            func=AF.Exp,
                                            scale=gA1[:, cidx:cidx + 1])
                    S.stage("scalar", _em(s_da, ["M_dl", "gA1"], [kda]))

                    def v_sc(e, dt=dt, n=n, da=da, cidx=cidx, hcur=hcur):
                        hdst = hcur[:, n, :] if n < NPOOL1 \
                            else m_hv[:, n - NPOOL1, :]
                        return e.tensor_tensor_scan(
                            out=hdst, data0=da, data1=m_ua[:, n, :],
                            initial=gst1[:, cidx:cidx + 1],
                            op0=ALU.mult, op1=ALU.add)
                    S.stage("vector", _em(v_sc, ["M_ua", kda, "gst1", kh],
                                          [kh, "gst1"]))

                def v_fin(e, dt=dt, yb=yb, hcur=hcur):
                    e.tensor_copy(
                        out=gst1[:, NS * dt:NS * dt + NPOOL1]
                        .rearrange("p (a c) -> p a c", c=1),
                        in_=hcur[:, :, CS - 1:CS])
                    e.tensor_copy(
                        out=gst1[:, NS * dt + NPOOL1:NS * dt + NS]
                        .rearrange("p (a c) -> p a c", c=1),
                        in_=m_hv[:, :, CS - 1:CS])
                    e.tensor_tensor(out=m_hv, in0=m_hv,
                                    in1=m_crep[:, NPOOL1:16, :], op=ALU.mult)
                    zf = m_hv.rearrange("p a c -> p (a c)")
                    e.tensor_tensor(out=zf[:, 0:3 * CS], in0=zf[:, 0:3 * CS],
                                    in1=zf[:, 3 * CS:6 * CS], op=ALU.add)
                    e.tensor_tensor(out=zf[:, 0:CS], in0=zf[:, 0:CS],
                                    in1=zf[:, CS:2 * CS], op=ALU.add)
                    e.tensor_tensor(out=zf[:, 0:CS], in0=zf[:, 0:CS],
                                    in1=zf[:, 2 * CS:3 * CS], op=ALU.add)
                    e.tensor_tensor(out=yb, in0=zf[:, 0:CS],
                                    in1=zf[:, 6 * CS:7 * CS], op=ALU.add)
                    return e.scalar_tensor_tensor(
                        out=yb, in0=m_cv[:, dt, :],
                        scalar=c_dp[:, dt:dt + 1], in1=yb,
                        op0=ALU.mult, op1=ALU.add)
                S.stage("vector", _em(v_fin, [kh, "M_rep", "M_cv", "gv",
                                              "gst1", ky, "M_sp"],
                                      [ky, "gst1", "M_sp"]))

                def g_con(e, dt=dt, yb=yb, hcur=hcur):
                    e.tensor_tensor(out=hcur, in0=hcur,
                                    in1=m_crep[:, 0:NPOOL1, :], op=ALU.mult)
                    hf = hcur.rearrange("p a c -> p (a c)")
                    e.tensor_tensor(out=hf[:, 0:4 * CS], in0=hf[:, 0:4 * CS],
                                    in1=hf[:, 4 * CS:8 * CS], op=ALU.add)
                    e.tensor_tensor(out=hf[:, 0:2 * CS], in0=hf[:, 0:2 * CS],
                                    in1=hf[:, 2 * CS:4 * CS], op=ALU.add)
                    e.tensor_tensor(out=hf[:, 0:CS], in0=hf[:, 0:CS],
                                    in1=hf[:, CS:2 * CS], op=ALU.add)
                    e.tensor_tensor(out=hf[:, 0:CS], in0=hf[:, 0:CS],
                                    in1=hf[:, 8 * CS:9 * CS], op=ALU.add)
                    return e.tensor_tensor(out=yb, in0=yb, in1=hf[:, 0:CS],
                                           op=ALU.add)
                S.stage("gpsimd", _em(g_con, [kh, ky, "M_rep"], [ky, kh]))
                dma_stage("scalar", [(sc_y1[:, dt, sl], yb)], [ky], ["sc_y1"])

        S.run(blk)

    with ExitStack() as _stk:
        h_nw2 = _stk.enter_context(nc.sbuf_tensor("n_w2", [128, 8, D], BF16))
        h_ny = _stk.enter_context(nc.sbuf_tensor("n_y", [128, 2, 8, C], F32))
        h_nv = _stk.enter_context(nc.sbuf_tensor("n_v16", [128, 2, 8, C], BF16))
        h_nyg = _stk.enter_context(nc.sbuf_tensor("n_yg", [128, 8, C], BF16))
        h_nrb = _stk.enter_context(nc.sbuf_tensor("n_rb", [128, 2, 8, C], F32))
        h_ns3 = _stk.enter_context(nc.sbuf_tensor("n_s3", [128, 8, C], F32))
        h_nps0 = _stk.enter_context(nc.psum_tensor("n_ps0", [128, C], F32))
        h_nps1 = _stk.enter_context(nc.psum_tensor("n_ps1", [128, C], F32))
        blk = _stk.enter_context(nc.Block())
        n_w2 = h_nw2.ap(); n_y = h_ny.ap(); n_v16 = h_nv.ap()
        n_yg = h_nyg.ap(); n_rb = h_nrb.ap(); n_s3 = h_ns3.ap()
        n_pss = [h_nps0.ap(), h_nps1.ap()]

        dma_stage("gpsimd", [(n_w2, l1_w2.rearrange("(a b) m -> b a m", b=128))],
                  [], ["n_w2"])

        for j in range(nch):
            sl = slice(j * C, (j + 1) * C)
            jp = j % 2
            ny_j = n_y[:, jp, :, :]
            nv_j = n_v16[:, jp, :, :]
            nrb_j = n_rb[:, jp, :, :]
            kny = "N_y%d" % jp
            knrb = "N_rb%d" % jp
            dma_stage("sync", [
                (ny_j, sc_y1[:, :, sl]),
                (nv_j, sc_v[:, :, sl]),
                (nrb_j, sc_xh1[:, :, sl]),
            ], ["sc_y1", "sc_v", "sc_xh1"], [kny, knrb])

            def v_g(e, ny_j=ny_j, nv_j=nv_j):
                return e.tensor_tensor(out=n_yg, in0=ny_j, in1=nv_j,
                                       op=ALU.mult)
            S.stage("vector", _em(v_g, [kny], ["N_yg"]))

            for m in range(8):
                ps = n_pss[m % 2]
                kps = "N_ps%d" % (m % 2)

                def t_mm(e, m=m, ps=ps):
                    last = None
                    for k in range(8):
                        last = e.matmul(ps, n_w2[:, k, m * 128:(m + 1) * 128],
                                        n_yg[:, k, :], start=(k == 0),
                                        stop=(k == 7))
                    return last
                S.stage("tensor", _em(t_mm, ["N_yg", "n_w2"], [kps]))

                def v_ev(e, m=m, ps=ps, nrb_j=nrb_j):
                    return e.scalar_tensor_tensor(out=n_s3[:, m, :], in0=ps,
                                                  scalar=c_w2b[:, m:m + 1],
                                                  in1=nrb_j[:, m, :],
                                                  op0=ALU.add, op1=ALU.add)
                S.stage("vector", _em(v_ev, [kps, knrb, "gv"], ["N_s3"]))

            dma_stage("scalar", [(sc_s3[:, :, sl], n_s3)], ["N_s3"], ["sc_s3"])

        S.run(blk)

    with ExitStack() as _stk:
        h_of1 = _stk.enter_context(nc.sbuf_tensor("o_f1", [128, 8, 3 * D], BF16))
        h_of2 = _stk.enter_context(nc.sbuf_tensor("o_f2", [128, 24, D], BF16))
        h_ox = _stk.enter_context(nc.sbuf_tensor("o_x", [128, 8, C], F32))
        h_osq = _stk.enter_context(nc.sbuf_tensor("o_sq", [128, 8, C], F32))
        h_oxn = _stk.enter_context(nc.sbuf_tensor("o_xn16", [128, 8, C], BF16))
        h_orows = _stk.enter_context(nc.sbuf_tensor("o_rows", [1, 2 * C], F32))
        h_omu = _stk.enter_context(nc.sbuf_tensor("o_mu", [1, C], F32))
        h_ovp = _stk.enter_context(nc.sbuf_tensor("o_vp", [1, C], F32))
        h_oq = _stk.enter_context(nc.sbuf_tensor("o_q", [1, C], F32))
        h_orep = _stk.enter_context(nc.sbuf_tensor("o_rep", [128, 2 * C], F32))
        h_oh16 = _stk.enter_context(nc.sbuf_tensor("o_h16", [128, 24, C], BF16))
        h_os2 = _stk.enter_context(nc.sbuf_tensor("o_s2", [128, 8, C], F32))
        h_osta = _stk.enter_context(nc.psum_tensor("o_sta", [1, C], F32))
        h_opbr0 = _stk.enter_context(nc.psum_tensor("o_pbr0", [128, C], F32))
        h_opbr1 = _stk.enter_context(nc.psum_tensor("o_pbr1", [128, C], F32))
        h_ostb = _stk.enter_context(nc.psum_tensor("o_stb", [1, C], F32))
        h_ops0 = _stk.enter_context(nc.psum_tensor("o_ps0", [128, C], F32))
        h_ops1 = _stk.enter_context(nc.psum_tensor("o_ps1", [128, C], F32))
        blk = _stk.enter_context(nc.Block())
        o_f1 = h_of1.ap(); o_f2 = h_of2.ap(); o_x = h_ox.ap()
        o_sq = h_osq.ap(); o_xh = o_sq; o_xn16 = h_oxn.ap()
        o_rows = h_orows.ap(); o_mu = h_omu.ap(); o_vp = h_ovp.ap()
        o_q = h_oq.ap(); o_f = h_ostb.ap(); o_rep = h_orep.ap()
        o_h16 = h_oh16.ap(); o_s2 = h_os2.ap(); o_s4 = o_sq
        o_pss = [h_ops0.ap(), h_ops1.ap()]
        lnbo = {"pbr": [h_opbr0.ap(), h_opbr1.ap()], "tag": "lnO", "sq": o_sq, "sta": h_osta.ap(), "stb": h_ostb.ap(),
                "mu": o_mu, "vp": o_vp, "q": o_q, "f": o_f,
                "rows": o_rows, "rep": o_rep}

        dma_stage("gpsimd", [
            (o_f1, l1_ff1.rearrange("(a b) m -> b a m", b=128)),
            (o_f2, l1_ff2.rearrange("(a b) m -> b a m", b=128)),
        ], [], ["o_f1", "o_f2"])

        for j in range(nch):
            sl = slice(j * C, (j + 1) * C)
            dma_stage("sync", [(o_x, sc_s3[:, :, sl]),
                               (o_s2, sc_s2[:, :, sl])],
                      ["sc_s3", "sc_s2"], ["O_x", "O_s2"])
            emit_ln("O", o_x, 8, lnbo, second=False)
            emit_norm("O", o_x, lnbo["pbr"], o_xh, 8, "lnO", xh_key="lnO_sq")

            def s_xn(e):
                last = None
                for dt in range(8):
                    last = e.activation(out=o_xn16[:, dt, :],
                                        in_=o_xh[:, dt, :],
                                        func=AF.Identity,
                                        scale=c_ln2g[:, dt:dt + 1],
                                        bias=c_ln2b[:, dt:dt + 1])
                return last
            S.stage("scalar", _em(s_xn, ["lnO_sq", "gv"], ["O_xn"]))

            for m in range(24):
                ps = o_pss[m % 2]
                kps = "O_ps%d" % (m % 2)

                def t_mm(e, m=m, ps=ps):
                    last = None
                    for k in range(8):
                        last = e.matmul(ps, o_f1[:, k, m * 128:(m + 1) * 128],
                                        o_xn16[:, k, :], start=(k == 0),
                                        stop=(k == 7))
                    return last
                S.stage("tensor", _em(t_mm, ["O_xn", "o_f1"], [kps]))

                def s_ev(e, m=m, ps=ps):
                    return e.activation(out=o_h16[:, m, :], in_=ps,
                                        func=AF.Gelu, scale=1.0,
                                        bias=c_ff1b[:, m:m + 1])
                S.stage("scalar", _em(s_ev, [kps, "gv"], ["O_h16"]))

            for m in range(8):
                ps = o_pss[m % 2]
                kps = "O_ps%d" % (m % 2)

                def t_mm(e, m=m, ps=ps):
                    last = None
                    for k in range(24):
                        last = e.matmul(ps, o_f2[:, k, m * 128:(m + 1) * 128],
                                        o_h16[:, k, :], start=(k == 0),
                                        stop=(k == 23))
                    return last
                S.stage("tensor", _em(t_mm, ["O_h16", "o_f2"], [kps]))

                def v_ev(e, m=m, ps=ps):
                    e.scalar_tensor_tensor(out=o_s4[:, m, :], in0=ps,
                                           scalar=c_ff2b[:, m:m + 1],
                                           in1=o_x[:, m, :],
                                           op0=ALU.add, op1=ALU.add)
                    return e.tensor_tensor(out=o_s4[:, m, :],
                                           in0=o_s4[:, m, :],
                                           in1=o_s2[:, m, :], op=ALU.add)
                S.stage("vector", _em(v_ev, [kps, "O_x", "O_s2", "gv"],
                                      ["lnO_sq"]))

            dma_stage("scalar", [(sc_s4[:, :, sl], o_s4)], ["lnO_sq"], ["sc_s4"])

        S.run(blk)

    with ExitStack() as _stk:
        h_zct = _stk.enter_context(nc.sbuf_tensor("z_ct", [128, 8, D], BF16))
        h_zst = _stk.enter_context(nc.sbuf_tensor("z_st", [128, 8, D], BF16))
        h_zwa = _stk.enter_context(nc.sbuf_tensor("z_wa", [128, 8, 512], F32))
        h_zwb = _stk.enter_context(nc.sbuf_tensor("z_wb", [128, 8, 512], F32))
        h_zx = _stk.enter_context(nc.sbuf_tensor("z_x", [128, 2, 8, C], F32))
        h_zsq = _stk.enter_context(nc.sbuf_tensor("z_sq", [128, 8, C], F32))
        h_zxh = _stk.enter_context(nc.sbuf_tensor("z_xh", [128, 8, C], F32))
        h_zx16 = _stk.enter_context(nc.sbuf_tensor("z_x16", [128, 8, C], BF16))
        h_zrows = _stk.enter_context(nc.sbuf_tensor("z_rows", [1, 2 * C], F32))
        h_zmu = _stk.enter_context(nc.sbuf_tensor("z_mu", [1, C], F32))
        h_zvp = _stk.enter_context(nc.sbuf_tensor("z_vp", [1, C], F32))
        h_zq = _stk.enter_context(nc.sbuf_tensor("z_q", [1, C], F32))
        h_zrep = _stk.enter_context(nc.sbuf_tensor("z_rep", [128, 2 * C], F32))
        h_zs = _stk.enter_context(nc.sbuf_tensor("z_s", [128, 8, C], F32))
        h_zt = _stk.enter_context(nc.sbuf_tensor("z_t", [128, 8, C], F32))
        h_zo = _stk.enter_context(nc.sbuf_tensor("z_o", [128, 8, C], F32))
        h_zsta = _stk.enter_context(nc.psum_tensor("z_sta", [1, C], F32))
        h_zpbr0 = _stk.enter_context(nc.psum_tensor("z_pbr0", [128, C], F32))
        h_zpbr1 = _stk.enter_context(nc.psum_tensor("z_pbr1", [128, C], F32))
        h_zstb = _stk.enter_context(nc.psum_tensor("z_stb", [1, C], F32))
        h_zps0 = _stk.enter_context(nc.psum_tensor("z_ps0", [128, C], F32))
        h_zps1 = _stk.enter_context(nc.psum_tensor("z_ps1", [128, C], F32))
        blk = _stk.enter_context(nc.Block())
        z_ct = h_zct.ap(); z_st = h_zst.ap()
        z_wa = h_zwa.ap(); z_wb = h_zwb.ap()
        z_x = h_zx.ap(); z_sq = h_zsq.ap(); z_xh = h_zxh.ap()
        z_x16 = h_zx16.ap(); z_rows = h_zrows.ap()
        z_mu = h_zmu.ap(); z_vp = h_zvp.ap(); z_q = h_zq.ap(); z_f = h_zstb.ap()
        z_rep = h_zrep.ap(); z_s = h_zs.ap(); z_t = h_zt.ap(); z_o = h_zo.ap()
        z_pss = [h_zps0.ap(), h_zps1.ap()]
        lnbz = {"pbr": [h_zpbr0.ap(), h_zpbr1.ap()], "tag": "lnZ", "sq": z_sq, "sta": h_zsta.ap(), "stb": h_zstb.ap(),
                "mu": z_mu, "vp": z_vp, "q": z_q, "f": z_f,
                "rows": z_rows, "rep": z_rep}

        for half in range(2):
            hsl = slice(half * 512, (half + 1) * 512)
            dma_stage("sync", [
                (z_wa, l1f_th.rearrange("(a b) m -> b a m", b=128)[:, :, hsl]),
                (z_wb, l1f_tn.rearrange("(a b) m -> b a m", b=128)[:, :, hsl]),
            ], [], ["Z_wa", "Z_wb"])

            def v_add(e):
                return e.tensor_tensor(out=z_wa.rearrange("p a c -> p (a c)"),
                                       in0=z_wa.rearrange("p a c -> p (a c)"),
                                       in1=z_wb.rearrange("p a c -> p (a c)"),
                                       op=ALU.add)
            S.stage("vector", _em(v_add, ["Z_wa", "Z_wb"], ["Z_wa"]))

            def s_cos(e, hsl=hsl):
                return e.activation(out=z_ct[:, :, hsl], in_=z_wa,
                                    func=AF.Sin, bias=K_HALFPI)
            S.stage("scalar", _em(s_cos, ["Z_wa"], ["z_ct"]))

            dma_stage("sync", [
                (z_wa, l1f_ph.rearrange("(a b) m -> b a m", b=128)[:, :, hsl]),
                (z_wb, l1f_pn.rearrange("(a b) m -> b a m", b=128)[:, :, hsl]),
            ], [], ["Z_wa", "Z_wb"])

            def v_add2(e):
                return e.tensor_tensor(out=z_wa.rearrange("p a c -> p (a c)"),
                                       in0=z_wa.rearrange("p a c -> p (a c)"),
                                       in1=z_wb.rearrange("p a c -> p (a c)"),
                                       op=ALU.add)
            S.stage("vector", _em(v_add2, ["Z_wa", "Z_wb"], ["Z_wa"]))

            def s_sin(e, hsl=hsl):
                return e.activation(out=z_st[:, :, hsl], in_=z_wa,
                                    func=AF.Sin)
            S.stage("scalar", _em(s_sin, ["Z_wa"], ["z_st"]))

        for j in range(nch):
            sl = slice(j * C, (j + 1) * C)
            jp = j % 2
            zx_j = z_x[:, jp, :, :]
            kzx = "Z_x%d" % jp
            dma_stage("sync", [(zx_j, sc_s4[:, :, sl])], ["sc_s4"], [kzx])
            emit_ln("Z", zx_j, 8, lnbz, second=False, xkey=kzx)
            emit_norm("Z", zx_j, lnbz["pbr"], z_xh, 8, "lnZ", xkey=kzx)

            def v_c16(e):
                return e.tensor_copy(out=z_x16.rearrange("p a c -> p (a c)"),
                                     in_=z_xh.rearrange("p a c -> p (a c)"))
            S.stage("vector", _em(v_c16, ["Z_xh"], ["Z_x16"]))

            for m in range(8):
                def t_re(e, m=m):
                    last = None
                    for k in range(8):
                        last = e.matmul(z_pss[0], z_ct[:, k, m * 128:(m + 1) * 128],
                                        z_x16[:, k, :], start=(k == 0),
                                        stop=(k == 7))
                    return last
                S.stage("tensor", _em(t_re, ["Z_x16", "z_ct"], ["Z_ps0"]))

                def t_im(e, m=m):
                    last = None
                    for k in range(8):
                        last = e.matmul(z_pss[1], z_st[:, k, m * 128:(m + 1) * 128],
                                        z_x16[:, k, :], start=(k == 0),
                                        stop=(k == 7))
                    return last
                S.stage("tensor", _em(t_im, ["Z_x16", "z_st"], ["Z_ps1"]))

                def s_sq0(e, m=m):
                    return e.activation(out=z_s[:, m, :], in_=z_pss[0],
                                        func=AF.Square)
                S.stage("scalar", _em(s_sq0, ["Z_ps0"], ["Z_s"]))

                def s_sq1(e, m=m):
                    return e.activation(out=z_t[:, m, :], in_=z_pss[1],
                                        func=AF.Square)
                S.stage("scalar", _em(s_sq1, ["Z_ps1"], ["Z_t"]))

                def v_mag(e, m=m):
                    e.tensor_tensor(out=z_s[:, m, :], in0=z_s[:, m, :],
                                    in1=z_t[:, m, :], op=ALU.add)
                    return e.tensor_scalar_max(out=z_s[:, m, :],
                                               in0=z_s[:, m, :],
                                               scalar1=1e-30)
                S.stage("vector", _em(v_mag, ["Z_s", "Z_t"], ["Z_s"]))

            def s_sqrt(e):
                return e.activation(out=z_t.rearrange("p a c -> p (a c)"),
                                    in_=z_s.rearrange("p a c -> p (a c)"),
                                    func=AF.Sqrt)
            S.stage("scalar", _em(s_sqrt, ["Z_s"], ["Z_t"]))

            def v_nt(e, zx_j=zx_j):
                fs = z_s.rearrange("p a c -> p (a c)")
                fq0 = z_t.rearrange("p a c -> p (a c)")
                fo = z_o.rearrange("p a c -> p (a c)")
                fx = zx_j.rearrange("p a c -> p (a c)")
                e.reciprocal(out=fo, in_=fq0)
                e.tensor_tensor(out=fo, in0=fo, in1=fs, op=ALU.mult)
                e.tensor_tensor(out=fo, in0=fo, in1=fq0, op=ALU.add)
                e.tensor_scalar_mul(out=fo, in0=fo, scalar1=0.5)
                return e.tensor_tensor(out=fo, in0=fo, in1=fx, op=ALU.add)
            S.stage("vector", _em(v_nt, ["Z_t", "Z_s", kzx], ["Z_o"]))

            dma_stage("scalar", [(out[:, :, sl], z_o)], ["Z_o"], ["out"])

        S.run(blk)

    return nc



def _vec_tiles(v, ntile):
    return np.ascontiguousarray(np.asarray(v, dtype=np.float32)
                                .reshape(ntile, 128).T)


def _fm(x):
    x = np.asarray(x, dtype=np.float32)
    Tt, F = x.shape
    return np.ascontiguousarray(x.T.reshape(F // 128, 128, Tt)
                                .transpose(1, 0, 2))


def _dropout_maskval(shape):
    import jax
    with jax.default_device(jax.devices("cpu")[0]):
        keep = jax.random.bernoulli(
            jax.random.key(42, impl="threefry2x32"), 0.8, shape)
        keep = np.asarray(keep)
    return np.where(keep, np.float32(1.25), np.float32(0.0))


def prep_inmaps(inputs, T):
    i = {k: np.asarray(v) for k, v in inputs.items()}
    B = i["L0_attn_x"].shape[0]
    mv = _dropout_maskval((B, T, D))
    base = {
        "l0_lnw": _vec_tiles(i["L0_attn_ln_w"], 8),
        "l0_lnb": _vec_tiles(i["L0_attn_ln_b"], 8),
        "l0_win": np.ascontiguousarray(i["L0_attn_W_in"], dtype=np.float32),
        "l0_bin": _vec_tiles(i["L0_attn_b_in"], 32),
        "l0_convw": np.ascontiguousarray(
            i["L0_attn_conv_w"].reshape(16, 128, 4).transpose(1, 0, 2)
            if i["L0_attn_conv_w"].ndim == 2 else
            i["L0_attn_conv_w"][:, 0, :].reshape(16, 128, 4).transpose(1, 0, 2),
            dtype=np.float32),
        "l0_convb": _vec_tiles(i["L0_attn_conv_b"], 16),
        "l0_alog": np.ascontiguousarray(
            i["L0_attn_A_log"].reshape(16, 128, NS).transpose(1, 0, 2),
            dtype=np.float32),
        "l0_d": _vec_tiles(i["L0_attn_D"], 16),
        "l0_wx": np.ascontiguousarray(i["L0_attn_W_x"], dtype=np.float32),
        "l0_bx": np.ascontiguousarray(i["L0_attn_b_x"].reshape(32, 1),
                                      dtype=np.float32),
        "l0_wout": np.ascontiguousarray(i["L0_attn_W_out"], dtype=np.float32),
        "l0_bout": _vec_tiles(i["L0_attn_b_out"], 8),
        "f0_w1t": np.ascontiguousarray(i["L0_ffn_W1"].T, dtype=np.float32),
        "f0_b1": _vec_tiles(i["L0_ffn_b1"], 32),
        "f0_w2t": np.ascontiguousarray(i["L0_ffn_W2"].T, dtype=np.float32),
        "f0_b2": _vec_tiles(i["L0_ffn_b2"], 8),
        "l1_ln1g": _vec_tiles(i["L1_attn_ln1_g"], 8),
        "l1_ln1b": _vec_tiles(i["L1_attn_ln1_b"], 8),
        "l1_lng": _vec_tiles(i["L1_attn_ln_g"], 8),
        "l1_lnb": _vec_tiles(i["L1_attn_ln_b"], 8),
        "l1_w1": np.ascontiguousarray(i["L1_attn_w1_w"], dtype=np.float32),
        "l1_w1b": _vec_tiles(i["L1_attn_w1_b"], 8),
        "l1_v1": np.ascontiguousarray(i["L1_attn_v1_w"], dtype=np.float32),
        "l1_v1b": _vec_tiles(i["L1_attn_v1_b"], 8),
        "l1_w2": np.ascontiguousarray(i["L1_attn_w2_w"], dtype=np.float32),
        "l1_w2b": _vec_tiles(i["L1_attn_w2_b"], 8),
        "l1_convw": np.ascontiguousarray(
            i["L1_attn_conv_w"][:, 0, :].reshape(8, 128, 3).transpose(1, 0, 2),
            dtype=np.float32),
        "l1_dtw": np.ascontiguousarray(i["L1_attn_dt_w"], dtype=np.float32),
        "l1_dtb": _vec_tiles(i["L1_attn_dt_b"], 8),
        "l1_bpw": np.ascontiguousarray(i["L1_attn_Bp_w"], dtype=np.float32),
        "l1_bpb": np.ascontiguousarray(i["L1_attn_Bp_b"].reshape(16, 1),
                                       dtype=np.float32),
        "l1_cpw": np.ascontiguousarray(i["L1_attn_Cp_w"], dtype=np.float32),
        "l1_cpb": np.ascontiguousarray(i["L1_attn_Cp_b"].reshape(16, 1),
                                       dtype=np.float32),
        "l1_alog": np.ascontiguousarray(
            i["L1_attn_A_log"].reshape(8, 128, NS).transpose(1, 0, 2),
            dtype=np.float32),
        "l1_dp": _vec_tiles(i["L1_attn_D_p"], 8),
        "l1_ln2g": _vec_tiles(i["L1_attn_ln2_g"], 8),
        "l1_ln2b": _vec_tiles(i["L1_attn_ln2_b"], 8),
        "l1_ff1": np.ascontiguousarray(i["L1_attn_ff1_w"], dtype=np.float32),
        "l1_ff1b": _vec_tiles(i["L1_attn_ff1_b"], 24),
        "l1_ff2": np.ascontiguousarray(i["L1_attn_ff2_w"], dtype=np.float32),
        "l1_ff2b": _vec_tiles(i["L1_attn_ff2_b"], 8),
        "l1f_th": np.ascontiguousarray(i["L1_ffn_theta"], dtype=np.float32),
        "l1f_ph": np.ascontiguousarray(i["L1_ffn_phi"], dtype=np.float32),
        "l1f_tn": np.ascontiguousarray(i["L1_ffn_theta_noise"], dtype=np.float32),
        "l1f_pn": np.ascontiguousarray(i["L1_ffn_phi_noise"], dtype=np.float32),
    }
    maps = []
    for core in range(8):
        b = (core // 2) % B
        m = dict(base)
        m["xin"] = _fm(i["L0_attn_x"][b])
        m["mask16"] = _fm(mv[b]).astype(ml_dtypes.bfloat16)
        m["bceye"] = np.ascontiguousarray(
            np.repeat(np.eye(16, dtype=np.float32), 128, axis=1)
        ).astype(ml_dtypes.bfloat16)
        maps.append(m)
    return maps


def kernel(**inputs):
    T = inputs["L0_attn_x"].shape[1]
    key = ("nc", T)
    if key not in _NC_CACHE:
        _NC_CACHE[key] = build_nc(T=T)
    nc = _NC_CACHE[key]
    maps = prep_inmaps(inputs, T)
    res = run_bass_kernel_spmd(nc, maps, core_ids=list(range(8)))
    B = inputs["L0_attn_x"].shape[0]
    outs = []
    for b in range(B):
        o = np.asarray(res.results[2 * b]["out"])
        outs.append(o.transpose(1, 0, 2).reshape(D, T).T)
    return np.stack(outs).astype(np.float32)
